# version 11 (final)
# baseline (speedup 1.0000x reference)
import math
import numpy as np
import ml_dtypes
import contextlib
import concourse.bass as bass
import concourse.mybir as mybir
from concourse.bass_utils import run_bass_kernel_spmd

F32 = mybir.dt.float32
BF16 = mybir.dt.bfloat16
AF = mybir.ActivationFunctionType
ALU = mybir.AluOpType
AX = mybir.AxisListType

D = 2048
DC = 16
PROJ_P = 8448
PC = 66
NCONV = 33
INTER = 4096
FH = 8192
AH = 16
AE = 128
ASCALE = 1.0 / math.sqrt(AE)
LN_EPS = 1e-5
GN_EPS = 1e-6
MASK_NEG = -30000.0

ALL_PHASES = ("const", "prep", "mamba", "ffn1", "attn", "ffn2")

CSHIFT = 8.0
CONST_VALUES = (LN_EPS, GN_EPS, -0.5, -1.0, ASCALE, -ASCALE, 1.0, 0.0, -CSHIFT)

ENGS = ("pe", "dve", "act", "pool")


class KB:
    def __init__(self, L=4096, phases=ALL_PHASES, dumps=()):
        assert L % 512 == 0
        self.L = L
        self.TB = L // 128
        self.TR = L // 512
        self.phases = phases
        self.dumps = dumps
        self.nc = bass.Bass("TRN2", target_bir_lowering=False)
        self.c = {}
        self.sem = {}
        self._seen = {k: {} for k in ("tensor", "vector", "scalar", "gpsimd", "sync")}
        self._declare_dram()

    def _declare_dram(self):
        nc, L = self.nc, self.L
        di = lambda n, s, dt=F32: nc.dram_tensor(n, s, dt, kind="ExternalInput")
        ds = lambda n, s, dt: nc.dram_tensor(n, s, dt, kind="Internal")
        self.x0 = di("x0", [L, D])
        self.winp = di("winp", [PROJ_P, D])
        self.cwt = di("cwt", [128, NCONV * 4])
        self.cbt = di("cbt", [128, NCONV])
        self.dtb = di("dtb", [64, 1])
        self.alog = di("alog", [64, 1])
        self.dvec = di("dvec", [64, 1])
        self.woutm = di("woutm", [D, INTER])
        self.wqkv = di("wqkv", [D, 3 * D])
        self.bqkvt = di("bqkvt", [128, 48])
        self.wouta = di("wouta", [D, D])
        self.boutbc = di("boutbc", [128, D])
        self.wup = di("wup", [FH, D])
        self.wdn = di("wdn", [D, FH])
        self.upsct = di("upsct", [128, 64])
        self.dsct = di("dsct", [128, 16])
        self.osct = di("osct", [128, 16])
        self.nwbc = di("nwbc", [128, INTER])
        self.idbf = di("idbf", [128, 128], BF16)
        self.idf = di("idf", [128, 128])
        self.ones1 = di("ones1", [1, 128], BF16)
        self.onec = di("onec", [128, 1], BF16)
        self.smlog = di("smlog", [128, 128])

        self.y_out = nc.dram_tensor("y_out", [L, D], F32, kind="ExternalOutput")

        self.s_lnT = ds("s_lnT", [DC, 128, L], BF16)
        self.s_winT = ds("s_winT", [PC, 128, D], BF16)
        self.s_woutT = ds("s_woutT", [32, 128, D], BF16)
        self.s_wupT = ds("s_wupT", [64, 128, D], BF16)
        self.s_wdnT = ds("s_wdnT", [16, 128, FH], BF16)
        self.s_wqkv = ds("s_wqkv", [48, 128, D], BF16)
        self.s_wouta = ds("s_wouta", [16, 128, D], BF16)
        self.s_projT = ds("s_projT", [PC, 128, L], BF16)
        self.s_dtT = ds("s_dtT", [64, L], F32)
        self.s_dtT2 = ds("s_dtT2", [64, L], F32)
        self.s_adtT = ds("s_adtT", [64, L], F32)
        self.s_xbcT = ds("s_xbcT", [NCONV, 128, L], BF16)
        self.s_y = ds("s_y", [self.TB, 128, INTER], F32)
        self.s_qkvT = ds("s_qkvT", [48, 128, L], BF16)
        self.s_aoT = ds("s_aoT", [AH, 128, L], BF16)
        self.s_x1 = ds("s_x1", [L, D], F32)
        self.s_x2 = ds("s_x2", [L, D], F32)
        self.s_x3 = ds("s_x3", [L, D], F32)
        self.s_rowA = ds("s_rowA", [1, 8192], F32)
        self.s_rowB = ds("s_rowB", [1, 8192], F32)
        self._dump_outs = {}
        for name in self.dumps:
            src = getattr(self, name)
            self._dump_outs[name] = nc.dram_tensor(
                "dump_" + name, list(src.shape), src.dtype, kind="ExternalOutput")

    def q(self, name):
        if name not in self.sem:
            self.sem[name] = self._sempool.pop()
            self.c[name] = 0
        return name

    def inc(self, instr, name, k=1):
        self.q(name)
        instr.then_inc(self.sem[name], k)
        self.c[name] += k
        return self.c[name]

    def dinc(self, instr, name):
        return self.inc(instr, name, 16)

    def wait(self, eng, kind, name, val):
        self.q(name)
        eng.wait_ge(self.sem[name], val)
        if val > self._seen[kind].get(name, 0):
            self._seen[kind][name] = val

    def barrier(self):
        self.snap_ = dict(self.c)
        return self.snap_

    _OWN = {
        "tensor": ("pe",), "vector": ("dve",), "scalar": ("act",),
        "gpsimd": ("pool",), "sync": (),
    }

    def bwait(self, eng, kind):
        seen = self._seen[kind]
        for n, v in self.snap_.items():
            if v > 0 and n not in self._OWN[kind] and seen.get(n, 0) < v:
                eng.wait_ge(self.sem[n], v)
                seen[n] = v

    _SEDGE = {"vector": "dvs", "scalar": "acs", "gpsimd": "pos", "tensor": "pes"}

    def edge(self, eng, kind, instr):
        n = self._SEDGE[kind]
        self.inc(instr, n)
        self.wait(eng, kind, n, self.c[n])

    def kcol(self, val, rows=128):
        i = self._cvals[val]
        return self.CONSTV[0:rows, i:i + 1]

    def build(self):
        nc = self.nc
        with contextlib.ExitStack() as ctx:
            E = ctx.enter_context
            sb = lambda n, s, dt: E(nc.sbuf_tensor(n, s, dt))
            self.c_idbf = sb("c_idbf", [128, 128], BF16)
            self.c_idf = sb("c_idf", [128, 128], F32)
            self.c_ones = sb("c_ones", [1, 128], BF16)
            self.c_onec = sb("c_onec", [128, 1], BF16)
            self.c_smlog = sb("c_smlog", [128, 128], F32)
            self.p_cwt = sb("p_cwt", [128, NCONV * 4], F32)
            self.p_cbt = sb("p_cbt", [128, NCONV], F32)
            self.p_dtb = sb("p_dtb", [64, 1], F32)
            self.p_alog = sb("p_alog", [64, 1], F32)
            self.p_A = sb("p_A", [64, 1], F32)
            self.p_dvec = sb("p_dvec", [64, 1], F32)
            self.p_drep = sb("p_drep", [128, 64], F32)
            self.p_bqkv = sb("p_bqkv", [128, 48], F32)
            self.p_upsc = sb("p_upsc", [128, 64], F32)
            self.p_dsc = sb("p_dsc", [128, 16], F32)
            self.p_osc = sb("p_osc", [128, 16], F32)
            self.p_dscc = sb("p_dscc", [128, 16], F32)
            self.CONSTV = sb("CONSTV", [128, len(CONST_VALUES)], F32)
            self._cvals = {v: i for i, v in enumerate(CONST_VALUES)}
            self.AR1 = sb("AR1", [128, 32768], BF16)
            self.AR2 = sb("AR2", [128, 17408], BF16)
            self.AF1 = sb("AF1", [128, 8192], F32)
            self.AF2 = sb("AF2", [128, 2048], F32)
            self.XT = sb("XT", [128, 2048], F32)
            self.BOUT = sb("BOUT", [128, 2048], F32)
            self.ST_S = sb("ST_S", [64, INTER], F32)
            self.ST_BF = sb("ST_BF", [64, INTER], BF16)
            self.SM1 = sb("SM1", [128, 64], F32)
            self.YTMP = sb("YTMP", [128, 512], F32)
            self.YTMP2 = sb("YTMP2", [128, 512], F32)
            self.YTMP3 = sb("YTMP3", [128, 512], F32)
            self.cs0 = sb("cs0", [64, 128], F32)
            self.uu = sb("uu", [64, 128], F32)
            self.wb = sb("wb", [64, 128], F32)
            self.dtwb = sb("dtwb", [64, 128], F32)
            self.wb2 = sb("wb2", [64, 128], F32)
            self.z64 = sb("z64", [64, 128], F32)
            self.dt_r = sb("dt_r", [64, 512], F32)
            self.adt_r = sb("adt_r", [64, 512], F32)
            self.uerep = sb("uerep", [64, 64], F32)
            self.cstok = sb("cstok", [128, 64], F32)
            self.dttok = sb("dttok", [128, 64], F32)
            self.dwtok = sb("dwtok", [128, 64], F32)
            self.utok = sb("utok", [128, 64], F32)
            self.ccb = sb("ccb", [64, 128], BF16)
            self.btok = sb("btok", [128, 64], BF16)
            self.sraw = sb("sraw", [128, 128], F32)
            self.rowsb = sb("rowsb", [1, 2048], F32)
            self.crep = sb("crep", [128, 512], F32)
            self.r4 = sb("r4", [4, 128], F32)
            self.PSQ = [E(nc.psum_tensor("PSQ%d" % i, [128, 1024], F32))
                        for i in range(4)]
            self.PSE = self.PSQ[0][:, :]
            self.PSA = self.PSQ[1][:, 0:512]
            self.PSB = self.PSQ[1][:, 512:1024]
            self.PSC = self.PSQ[2][:, 0:512]
            self.PSF = self.PSQ[2][:, 512:1024]
            self.PSG = self.PSQ[3][:, 0:512]
            self.PSH = self.PSQ[3][:, 512:1024]
            self.PS4 = [self.PSA, self.PSB, self.PSC, self.PSF]
            self._sempool = [E(nc.semaphore("s%d" % i)) for i in range(90)][::-1]
            for n in ENGS:
                self.q(n)
            self.block = E(nc.Block())

            ph = self.phases
            if "const" in ph:
                self.emit_const()
            if "prep" in ph:
                self.emit_prep()
            if "mamba" in ph:
                self.emit_ln(self.x0)
                self.emit_minproj()
                self.emit_mconv()
                self.emit_mssd()
                self.emit_m5(self.x0, self.s_x1)
            if "ffn1" in ph:
                self.emit_ln(self.s_x1)
                self.emit_ffn(self.s_x1, self.s_x2)
            if "attn" in ph:
                self.emit_ln(self.s_x2)
                self.emit_attn(self.s_x2, self.s_x3)
            if "ffn2" in ph:
                self.emit_ln(self.s_x3)
                self.emit_ffn(self.s_x3, self.y_out)
            if self.dumps:
                self.emit_dumps()
            self.emit_finish()
        return self.nc

    def emit_const(self):
        k = self
        k.barrier()

        @k.block.sync
        def _(q):
            for dst, src in (
                (k.c_idbf, k.idbf), (k.c_idf, k.idf), (k.c_ones, k.ones1),
                (k.c_onec, k.onec),
                (k.c_smlog, k.smlog), (k.p_cwt, k.cwt), (k.p_cbt, k.cbt),
                (k.p_dtb, k.dtb), (k.p_alog, k.alog), (k.p_dvec, k.dvec),
                (k.p_bqkv, k.bqkvt), (k.p_upsc, k.upsct), (k.p_dsc, k.dsct),
                (k.p_osc, k.osct), (k.BOUT, k.boutbc),
            ):
                k.dinc(q.dma_start(out=dst[:], in_=src.ap()), "qc")

        nload = k.c["qc"]

        @k.block.scalar
        def _(a):
            k.wait(a, "scalar", "qc", nload)
            k.inc(a.activation(out=k.p_A[:], in_=k.p_alog[:], func=AF.Exp), "act")

        av = k.c["act"]

        @k.block.vector
        def _(v):
            for val, i in k._cvals.items():
                v.memset(k.CONSTV[:, i:i + 1], float(val))
            v.memset(k.z64[:], 0.0)
            k.wait(v, "vector", "act", av)
            k.inc(v.tensor_scalar_mul(k.p_A[:], k.p_A[:], -1.0), "dve")
            k.inc(v.tensor_tensor(out=k.p_dscc[:], in0=k.p_dsc[:],
                                  in1=k.p_osc[:], op=ALU.mult), "dve")

        @k.block.tensor
        def _(t):
            k.wait(t, "tensor", "qc", nload)
            k.inc(t.transpose(k.PSG[0:1, 0:64], k.p_dvec[:], k.c_idf[0:64, 0:64]), "pe")

        pv = k.c["pe"]

        @k.block.vector
        def _(v):
            k.wait(v, "vector", "pe", pv)
            k.inc(v.tensor_copy(k.rowsb[0:1, 0:64], k.PSG[0:1, 0:64]), "dve")

        dv2 = k.c["dve"]

        @k.block.gpsimd
        def _(g):
            k.wait(g, "gpsimd", "dve", dv2)
            k.dinc(g.dma_start(out=k.s_rowA.ap()[0:1, 0:64], in_=k.rowsb[0:1, 0:64]), "qrA")
            k.wait(g, "gpsimd", "qrA", k.c["qrA"])
            k.dinc(g.dma_start(out=k.p_drep[:],
                               in_=bass.AP(k.s_rowA, 0, [[0, 128], [1, 64]])), "qrA")

    def emit_prep(self):
        k = self
        self._prep_transpose(k.winp, PROJ_P, D, k.s_winT, sign=False)
        self._prep_transpose(k.wup, FH, D, k.s_wupT, sign=True)
        self._prep_transpose(k.wdn, D, FH, k.s_wdnT, sign=True)
        self._prep_transpose_col(k.woutm, D, INTER, k.s_woutT)
        self._prep_cast_qkv()
        self._prep_cast_rows(k.wouta, D, D, k.s_wouta)

    def _prep_transpose(self, src, R, C, dst, sign):
        k = self
        nj, ncch = R // 128, C // 128
        nslot = 2 if C <= 4096 else 1
        s = k.barrier()
        pb, ab = s.get("pe", 0), s.get("act", 0)
        qpw0 = [s.get("qpw0", 0), s.get("qpw1", 0)]
        qps0 = [s.get("qps0", 0), s.get("qps1", 0)]

        def lslot(j):
            return k.AF1[:, (j % nslot) * 4096:(j % nslot) * 4096 + C]

        def sslot(j):
            return k.AR2[:, (j % 2) * 8192:(j % 2) * 8192 + C]

        @k.block.sync
        def _(q):
            k.bwait(q, "sync")
            for j in range(nj):
                if j >= nslot:
                    k.wait(q, "sync", "pe", pb + (j - nslot + 1) * ncch)
                k.dinc(q.dma_start(out=lslot(j),
                                   in_=src.ap()[j * 128:(j + 1) * 128, :]),
                       "qpw%d" % (j % 2))

        @k.block.tensor
        def _(t):
            k.bwait(t, "tensor")
            for j in range(nj):
                k.wait(t, "tensor", "qpw%d" % (j % 2), qpw0[j % 2] + 16 * (j // 2 + 1))
                for c in range(ncch):
                    g = j * ncch + c
                    if g >= 4:
                        k.wait(t, "tensor", "act", ab + g - 3)
                    ps = k.PS4[g % 4]
                    k.inc(t.transpose(ps[:, 0:128], lslot(j)[:, c * 128:(c + 1) * 128],
                                      k.c_idf[:]), "pe")

        @k.block.scalar
        def _(a):
            k.bwait(a, "scalar")
            fn = AF.Sign if sign else AF.Copy
            for j in range(nj):
                for c in range(ncch):
                    g = j * ncch + c
                    k.wait(a, "scalar", "pe", pb + g + 1)
                    if c == 0 and j >= 2:
                        k.wait(a, "scalar", "qps%d" % (j % 2), qps0[j % 2] + 16 * (j // 2))
                    ps = k.PS4[g % 4]
                    k.inc(a.activation(out=sslot(j)[:, c * 128:(c + 1) * 128],
                                       in_=ps[:, 0:128], func=fn), "act")

        @k.block.gpsimd
        def _(g_):
            k.bwait(g_, "gpsimd")
            for j in range(nj):
                k.wait(g_, "gpsimd", "act", ab + (j + 1) * ncch)
                k.dinc(g_.dma_start(out=dst.ap()[j], in_=sslot(j)), "qps%d" % (j % 2))

    def _prep_transpose_col(self, src, R, C, dst):
        k = self
        ni, nrch = C // 128, R // 128
        s = k.barrier()
        pb, ab = s.get("pe", 0), s.get("act", 0)
        qpw0 = [s.get("qpw0", 0), s.get("qpw1", 0)]
        qps0 = [s.get("qps0", 0), s.get("qps1", 0)]

        def lslot(i):
            return k.AF1[:, (i % 2) * 4096:(i % 2) * 4096 + R]

        def sslot(i):
            return k.AR2[:, (i % 2) * 8192:(i % 2) * 8192 + R]

        @k.block.sync
        def _(q):
            k.bwait(q, "sync")
            for i in range(ni):
                if i >= 2:
                    k.wait(q, "sync", "pe", pb + (i - 1) * nrch)
                ap = src.ap()[:, i * 128:(i + 1) * 128].rearrange("(rj p) f -> p rj f", p=128)
                half = nrch // 2
                k.dinc(q.dma_start(
                    out=lslot(i)[:, 0:half * 128].rearrange("p (rj f) -> p rj f", f=128),
                    in_=ap[:, 0:half]), "qpw%d" % (i % 2))
                k.dinc(q.dma_start(
                    out=lslot(i)[:, half * 128:nrch * 128].rearrange("p (rj f) -> p rj f", f=128),
                    in_=ap[:, half:nrch]), "qpw%d" % (i % 2))

        @k.block.tensor
        def _(t):
            k.bwait(t, "tensor")
            for i in range(ni):
                k.wait(t, "tensor", "qpw%d" % (i % 2), qpw0[i % 2] + 32 * (i // 2 + 1))
                for rj in range(nrch):
                    g = i * nrch + rj
                    if g >= 4:
                        k.wait(t, "tensor", "act", ab + g - 3)
                    ps = k.PS4[g % 4]
                    k.inc(t.transpose(ps[:, 0:128], lslot(i)[:, rj * 128:(rj + 1) * 128],
                                      k.c_idf[:]), "pe")

        @k.block.scalar
        def _(a):
            k.bwait(a, "scalar")
            for i in range(ni):
                for rj in range(nrch):
                    g = i * nrch + rj
                    k.wait(a, "scalar", "pe", pb + g + 1)
                    if rj == 0 and i >= 2:
                        k.wait(a, "scalar", "qps%d" % (i % 2), qps0[i % 2] + 16 * (i // 2))
                    ps = k.PS4[g % 4]
                    k.inc(a.activation(out=sslot(i)[:, rj * 128:(rj + 1) * 128],
                                       in_=ps[:, 0:128], func=AF.Copy), "act")

        @k.block.gpsimd
        def _(g_):
            k.bwait(g_, "gpsimd")
            for i in range(ni):
                k.wait(g_, "gpsimd", "act", ab + (i + 1) * nrch)
                k.dinc(g_.dma_start(out=dst.ap()[i], in_=sslot(i)), "qps%d" % (i % 2))

    def _prep_cast_qkv(self):
        k = self
        s = k.barrier()
        db = s.get("dve", 0)
        qpw0 = s.get("qpw0", 0)
        qps0 = s.get("qps0", 0)

        @k.block.sync
        def _(q):
            k.bwait(q, "sync")
            for c in range(DC):
                if c >= 1:
                    k.wait(q, "sync", "dve", db + c)
                k.dinc(q.dma_start(out=k.AF1[:, 0:6144],
                                   in_=k.wqkv.ap()[c * 128:(c + 1) * 128, :]), "qpw0")

        @k.block.vector
        def _(v):
            k.bwait(v, "vector")
            for c in range(DC):
                k.wait(v, "vector", "qpw0", qpw0 + 16 * (c + 1))
                if c >= 2:
                    k.wait(v, "vector", "qps0", qps0 + 16 * 8 * (c - 1))
                k.inc(v.tensor_copy(k.AR2[:, (c % 2) * 8192:(c % 2) * 8192 + 6144],
                                    k.AF1[:, 0:6144]), "dve")

        @k.block.gpsimd
        def _(g):
            k.bwait(g, "gpsimd")
            for c in range(DC):
                k.wait(g, "gpsimd", "dve", db + c + 1)
                if c >= 1:
                    k.wait(g, "gpsimd", "qps0", k.c["qps0"])
                for j0 in range(0, 48, 6):
                    dstv = k.s_wqkv.ap()[j0:j0 + 6, :, c * 128:(c + 1) * 128].transpose([1, 0, 2])
                    srcv = k.AR2[:, (c % 2) * 8192 + j0 * 128:(c % 2) * 8192 + (j0 + 6) * 128] \
                        .rearrange("p (j f) -> p j f", f=128)
                    k.dinc(g.dma_start(out=dstv, in_=srcv), "qps0")

    def _prep_cast_rows(self, src, R, C, dst):
        k = self
        s = k.barrier()
        db = s.get("dve", 0)
        qpw0 = s.get("qpw0", 0)
        qps0 = s.get("qps0", 0)

        qpw1 = s.get("qpw1", 0)
        qpwb = [qpw0, qpw1]

        @k.block.sync
        def _(q):
            k.bwait(q, "sync")
            for i in range(R // 128):
                if i >= 2:
                    k.wait(q, "sync", "dve", db + i - 1)
                k.dinc(q.dma_start(out=k.AF1[:, (i % 2) * 4096:(i % 2) * 4096 + C],
                                   in_=src.ap()[i * 128:(i + 1) * 128, :]),
                       "qpw%d" % (i % 2))

        @k.block.vector
        def _(v):
            k.bwait(v, "vector")
            for i in range(R // 128):
                k.wait(v, "vector", "qpw%d" % (i % 2), qpwb[i % 2] + 16 * (i // 2 + 1))
                if i >= 2:
                    k.wait(v, "vector", "qps0", qps0 + 16 * (i - 1))
                k.inc(v.tensor_copy(k.AR2[:, (i % 2) * 8192:(i % 2) * 8192 + C],
                                    k.AF1[:, (i % 2) * 4096:(i % 2) * 4096 + C]), "dve")

        @k.block.gpsimd
        def _(g):
            k.bwait(g, "gpsimd")
            for i in range(R // 128):
                k.wait(g, "gpsimd", "dve", db + i + 1)
                if i >= 1:
                    k.wait(g, "gpsimd", "qps0", k.c["qps0"])
                k.dinc(g.dma_start(out=dst.ap()[i],
                                   in_=k.AR2[:, (i % 2) * 8192:(i % 2) * 8192 + C]), "qps0")

    def emit_ln(self, x_src):
        k = self
        TB = k.TB
        s = k.barrier()
        d0, a0, p0 = s.get("dve", 0), s.get("act", 0), s.get("pe", 0)
        f0 = s.get("qfl", 0)
        x0s = [s.get("qx0", 0), s.get("qx1", 0)]
        pse_bf = k.PSE[:].bitcast(BF16)

        @k.block.sync
        def _(q):
            k.bwait(q, "sync")
            for tb in range(TB):
                if tb >= 2:
                    k.wait(q, "sync", "dve", d0 + 5 * (tb - 2) + 3)
                xt = (k.XT if tb % 2 == 0 else k.AF2)[:, 0:2048]
                k.dinc(q.dma_start(out=xt, in_=x_src.ap()[tb * 128:(tb + 1) * 128, :]),
                       "qx%d" % (tb % 2))

        @k.block.scalar
        def _(a):
            k.bwait(a, "scalar")
            for tb in range(TB):
                xt = (k.XT if tb % 2 == 0 else k.AF2)[:, 0:2048]
                r8 = (tb % 8) * 8
                sq = k.SM1[:, r8 + 2:r8 + 3]
                var = k.SM1[:, r8 + 3:r8 + 4]
                rstd = k.SM1[:, r8 + 4:r8 + 5]
                k.wait(a, "scalar", "qx%d" % (tb % 2), x0s[tb % 2] + 16 * (tb // 2 + 1))
                k.inc(a.activation(out=k.AR1[:, 4096:6144], in_=xt, func=AF.Square,
                                   accum_out=sq), "act")
                k.wait(a, "scalar", "dve", d0 + 5 * tb + 2)
                ln_at = k.inc(a.activation(out=var, in_=var, func=AF.Ln, bias=k.kcol(LN_EPS)), "act")
                k.wait(a, "scalar", "act", ln_at)
                k.inc(a.activation(out=rstd, in_=var, func=AF.Exp, scale=k.kcol(-0.5)), "act")

        @k.block.vector
        def _(v):
            k.bwait(v, "vector")
            for tb in range(TB):
                xt = (k.XT if tb % 2 == 0 else k.AF2)[:, 0:2048]
                r8 = (tb % 8) * 8
                ssum = k.SM1[:, r8 + 0:r8 + 1]
                m = k.SM1[:, r8 + 1:r8 + 2]
                sq = k.SM1[:, r8 + 2:r8 + 3]
                var = k.SM1[:, r8 + 3:r8 + 4]
                rstd = k.SM1[:, r8 + 4:r8 + 5]
                msq = k.SM1[:, r8 + 5:r8 + 6]
                k.wait(v, "vector", "qx%d" % (tb % 2), x0s[tb % 2] + 16 * (tb // 2 + 1))
                k.edge(v, "vector", v.tensor_reduce(out=ssum, in_=xt, axis=AX.X, op=ALU.add))
                m_at = k.inc(v.tensor_scalar_mul(m, ssum, 1.0 / D), "dve")
                k.wait(v, "vector", "dve", m_at)
                k.edge(v, "vector", v.tensor_scalar_mul(msq, m, m))
                k.wait(v, "vector", "act", a0 + 3 * tb + 1)
                k.inc(v.tensor_scalar(out=var, in0=sq, scalar1=1.0 / D, scalar2=msq,
                                      op0=ALU.mult, op1=ALU.subtract), "dve")
                k.wait(v, "vector", "act", a0 + 3 * tb + 3)
                if tb >= 2:
                    k.wait(v, "vector", "pe", p0 + 16 * (tb - 1))
                lnb = k.AR1[:, (tb % 2) * 2048:(tb % 2) * 2048 + 2048]
                k.inc(v.tensor_scalar(out=lnb, in0=xt, scalar1=m, scalar2=rstd,
                                      op0=ALU.subtract, op1=ALU.mult), "dve")
                for h in range(2):
                    k.wait(v, "vector", "pe", p0 + 16 * tb + 8 * (h + 1))
                    if tb % 4 == 0 and tb > 0 and h == 0:
                        k.wait(v, "vector", "qfl", f0 + 256 * (tb // 4))
                    st = k.AR2[:, 0:8192].rearrange("p (c f) -> p c f", f=512)
                    src_v = pse_bf[:, h * 1024:(h + 1) * 1024].rearrange(
                        "p (c f) -> p c f", f=128)
                    k.inc(v.tensor_copy(
                        st[:, h * 8:(h + 1) * 8, (tb % 4) * 128:(tb % 4) * 128 + 128],
                        src_v), "dve")

        @k.block.tensor
        def _(t):
            k.bwait(t, "tensor")
            for tb in range(TB):
                k.wait(t, "tensor", "dve", d0 + 5 * tb + 3)
                lnb = k.AR1[:, (tb % 2) * 2048:(tb % 2) * 2048 + 2048]
                for h in range(2):
                    if tb > 0 or h > 0:
                        prev_evac = (d0 + 5 * tb + 4) if h == 1 else (d0 + 5 * (tb - 1) + 5)
                        k.wait(t, "tensor", "dve", prev_evac)
                    for cth in range(8):
                        cc = h * 8 + cth
                        k.inc(t.transpose(
                            pse_bf[:, h * 1024 + cth * 128:h * 1024 + (cth + 1) * 128],
                            lnb[:, cc * 128:(cc + 1) * 128], k.c_idbf[:]), "pe")

        @k.block.gpsimd
        def _(g):
            k.bwait(g, "gpsimd")
            for grp in range(TB // 4):
                k.wait(g, "gpsimd", "dve", d0 + 5 * (4 * grp + 4))
                for ch in range(DC):
                    st = k.AR2[:, ch * 512:(ch + 1) * 512]
                    k.dinc(g.dma_start(
                        out=k.s_lnT.ap()[ch][:, grp * 512:(grp + 1) * 512], in_=st), "qfl")

    def emit_minproj(self):
        k = self
        s = k.barrier()
        d0, p0 = s.get("dve", 0), s.get("pe", 0)
        l0 = s.get("qlnr", 0)
        w0s = [s.get("qws0", 0), s.get("qws1", 0)]
        st0s = [s.get("qst0", 0), s.get("qst1", 0), s.get("qst2", 0)]
        dt0 = s.get("qdt", 0)
        NJ = PC

        def devac(n):
            return n + n // NJ

        @k.block.sync
        def _(q):
            k.bwait(q, "sync")
            for tr in range(k.TR):
                if tr > 0:
                    k.wait(q, "sync", "pe", p0 + 16 * NJ * tr)
                for c in range(DC):
                    k.dinc(q.dma_start(
                        out=k.AR2[:, c * 512:(c + 1) * 512],
                        in_=k.s_lnT.ap()[c][:, tr * 512:(tr + 1) * 512]), "qlnr")

        @k.block.scalar
        def _(a):
            k.bwait(a, "scalar")
            for tr in range(k.TR):
                for j in range(NJ):
                    i = tr * NJ + j
                    if i >= 2:
                        k.wait(a, "scalar", "pe", p0 + 16 * (i - 1))
                    ws = k.AR2[:, 8192 + (i % 2) * 2048:8192 + (i % 2) * 2048 + 2048]
                    k.dinc(a.dma_start(out=ws, in_=k.s_winT.ap()[j]), "qws%d" % (i % 2))

        @k.block.tensor
        def _(t):
            k.bwait(t, "tensor")
            for tr in range(k.TR):
                k.wait(t, "tensor", "qlnr", l0 + 16 * DC * (tr + 1))
                for j in range(NJ):
                    i = tr * NJ + j
                    k.wait(t, "tensor", "qws%d" % (i % 2), w0s[i % 2] + 16 * (i // 2 + 1))
                    if i >= 3:
                        k.wait(t, "tensor", "dve", d0 + devac(i - 2))
                    ws = k.AR2[:, 8192 + (i % 2) * 2048:8192 + (i % 2) * 2048 + 2048]
                    ps = k.PS4[i % 3]
                    for c in range(DC):
                        k.inc(t.matmul(ps[:], ws[:, c * 128:(c + 1) * 128],
                                       k.AR2[:, c * 512:(c + 1) * 512],
                                       start=(c == 0), stop=(c == DC - 1)), "pe")

        @k.block.vector
        def _(v):
            k.bwait(v, "vector")
            for tr in range(k.TR):
                for j in range(NJ):
                    i = tr * NJ + j
                    k.wait(v, "vector", "pe", p0 + 16 * (i + 1))
                    if i >= 3:
                        sl = i % 3
                        k.wait(v, "vector", "qst%d" % sl,
                               st0s[sl] + 16 * ((i - 3) // 3 + 1))
                    if j == NJ - 1 and tr >= 2:
                        k.wait(v, "vector", "qdt", dt0 + 16 * tr)
                    ps = k.PS4[i % 3]
                    stg = k.AR1[:, (i % 3) * 512:(i % 3) * 512 + 512]
                    k.inc(v.tensor_copy(stg, ps[:]), "dve")
                    if j == NJ - 1:
                        k.inc(v.tensor_copy(
                            k.AF2[0:64, (tr % 2) * 512:(tr % 2) * 512 + 512],
                            ps[0:64, :]), "dve")

        @k.block.gpsimd
        def _(g):
            k.bwait(g, "gpsimd")
            for tr in range(k.TR):
                for j in range(NJ):
                    i = tr * NJ + j
                    k.wait(g, "gpsimd", "dve", d0 + devac(i + 1))
                    stg = k.AR1[:, (i % 3) * 512:(i % 3) * 512 + 512]
                    k.dinc(g.dma_start(
                        out=k.s_projT.ap()[j][:, tr * 512:(tr + 1) * 512], in_=stg),
                        "qst%d" % (i % 3))
                    if j == NJ - 1:
                        k.dinc(g.dma_start(
                            out=k.s_dtT.ap()[:, tr * 512:(tr + 1) * 512],
                            in_=k.AF2[0:64, (tr % 2) * 512:(tr % 2) * 512 + 512]), "qdt")

    def emit_mconv(self):
        k = self
        L = k.L
        s = k.barrier()
        d0, a0, o0 = s.get("dve", 0), s.get("act", 0), s.get("pool", 0)
        cv0 = [s.get("qcv0", 0), s.get("qcv1", 0)]
        cs0_ = [s.get("qcs0", 0), s.get("qcs1", 0)]

        @k.block.sync
        def _(q):
            k.bwait(q, "sync")
            for ch in range(NCONV):
                if ch >= 2:
                    k.wait(q, "sync", "dve", d0 + ch - 1)
                k.dinc(q.dma_start(out=k.AR2[:, (ch % 2) * 4096:(ch % 2) * 4096 + L],
                                   in_=k.s_projT.ap()[32 + ch]), "qcv%d" % (ch % 2))

        def conv_ops(eng, kind, ch):
            half = ch % 2
            xin = k.AR2[:, (ch % 2) * 4096:(ch % 2) * 4096 + L]
            acc = k.AF1[:, half * 4096:half * 4096 + L]
            w = lambda kk: k.p_cwt[:, ch * 4 + kk:ch * 4 + kk + 1]
            k.wait(eng, kind, "qcv%d" % (ch % 2), cv0[ch % 2] + 16 * (ch // 2 + 1))
            if ch > 1:
                k.wait(eng, kind, "act", a0 + ch - 1)
            k.edge(eng, kind, eng.tensor_scalar(
                out=acc, in0=xin, scalar1=w(3), scalar2=k.p_cbt[:, ch:ch + 1],
                op0=ALU.mult, op1=ALU.add))
            k.edge(eng, kind, eng.scalar_tensor_tensor(
                out=acc[:, 1:L], in0=xin[:, 0:L - 1], scalar=w(2),
                in1=acc[:, 1:L], op0=ALU.mult, op1=ALU.add))
            k.edge(eng, kind, eng.scalar_tensor_tensor(
                out=acc[:, 2:L], in0=xin[:, 0:L - 2], scalar=w(1),
                in1=acc[:, 2:L], op0=ALU.mult, op1=ALU.add))
            k.inc(eng.scalar_tensor_tensor(
                out=acc[:, 3:L], in0=xin[:, 0:L - 3], scalar=w(0),
                in1=acc[:, 3:L], op0=ALU.mult, op1=ALU.add), "dve")

        @k.block.vector
        def _(v):
            k.bwait(v, "vector")
            for ch in range(NCONV):
                conv_ops(v, "vector", ch)

        @k.block.scalar
        def _(a):
            k.bwait(a, "scalar")
            for ch in range(NCONV):
                half = ch % 2
                acc = k.AF1[:, half * 4096:half * 4096 + L]
                stg = k.AR1[:, (ch % 2) * 4096:(ch % 2) * 4096 + L]
                k.wait(a, "scalar", "dve", d0 + ch + 1)
                if ch >= 2:
                    k.wait(a, "scalar", "qcs%d" % (ch % 2), cs0_[ch % 2] + 16 * (ch // 2))
                k.inc(a.activation(out=stg, in_=acc, func=AF.Silu), "act")

        @k.block.gpsimd
        def _(g):
            k.bwait(g, "gpsimd")
            for ch in range(NCONV):
                stg = k.AR1[:, (ch % 2) * 4096:(ch % 2) * 4096 + L]
                k.wait(g, "gpsimd", "act", a0 + ch + 1)
                k.dinc(g.dma_start(out=k.s_xbcT.ap()[ch], in_=stg), "qcs%d" % (ch % 2))

        s = k.barrier()

        @k.block.sync
        def _(q):
            k.bwait(q, "sync")

        for c0 in range(0, L, 2048):
            s = k.barrier()
            cw = min(2048, L - c0)

            @k.block.sync
            def _(q):
                k.bwait(q, "sync")
                k.dinc(q.dma_start(out=k.AF2[0:64, 0:cw],
                                   in_=k.s_dtT.ap()[:, c0:c0 + cw]), "qdl")

            qv = k.c["qdl"]

            @k.block.vector
            def _(v):
                k.bwait(v, "vector")
                k.wait(v, "vector", "qdl", qv)
                k.inc(v.tensor_scalar_add(k.AF2[0:64, 0:cw], k.AF2[0:64, 0:cw],
                                          k.p_dtb[:]), "dve")

            dv = k.c["dve"]

            @k.block.scalar
            def _(a):
                k.bwait(a, "scalar")
                k.wait(a, "scalar", "dve", dv)
                t = k.XT[0:64, 0:cw]
                k.edge(a, "scalar", a.activation(out=t, in_=k.AF2[0:64, 0:cw], func=AF.Abs))
                k.edge(a, "scalar", a.activation(out=k.AF1[0:64, 0:cw], in_=t, func=AF.Exp,
                                                 scale=k.kcol(-1.0, rows=64)))
                k.inc(a.activation(out=t, in_=k.AF1[0:64, 0:cw], func=AF.Ln,
                                   bias=k.kcol(1.0, rows=64)), "act")

            av = k.c["act"]

            @k.block.vector
            def _(v):
                k.bwait(v, "vector")
                k.wait(v, "vector", "act", av)
                k.edge(v, "vector", v.scalar_tensor_tensor(
                    out=k.AF2[0:64, 0:cw], in0=k.AF2[0:64, 0:cw], scalar=0.0,
                    in1=k.XT[0:64, 0:cw], op0=ALU.max, op1=ALU.add))
                k.inc(v.tensor_scalar_mul(k.XT[0:64, 0:cw], k.AF2[0:64, 0:cw],
                                          k.p_A[:]), "dve")

            dv2 = k.c["dve"]

            @k.block.gpsimd
            def _(g):
                k.bwait(g, "gpsimd")
                k.wait(g, "gpsimd", "dve", dv2)
                k.dinc(g.dma_start(out=k.s_dtT2.ap()[:, c0:c0 + cw],
                                   in_=k.AF2[0:64, 0:cw]), "qdl2")
                k.dinc(g.dma_start(out=k.s_adtT.ap()[:, c0:c0 + cw],
                                   in_=k.XT[0:64, 0:cw]), "qdl2")

    def emit_mssd(self):
        k = self
        L = k.L
        pse_bf = k.PSE[:].bitcast(BF16)

        k.barrier()

        @k.block.gpsimd
        def _(g):
            k.bwait(g, "gpsimd")
            k.inc(g.memset(k.ST_S[:], 0.0), "pool")

        @k.block.vector
        def _(v):
            k.bwait(v, "vector")
            k.inc(v.memset(k.ST_BF[:], 0.0), "dve")

        k._sp = None
        for rg in range(L // 512):
            if rg == 0:
                k.barrier()
            sp = k._sp

            @k.block.sync
            def _(q):
                if rg == 0:
                    k.bwait(q, "sync")
                else:
                    k.wait(q, "sync", "pe", sp["pe"])
                    k.wait(q, "sync", "pool", sp["pool"])
                    k.wait(q, "sync", "qrg", sp["qrg"])
                for ch in range(32):
                    k.dinc(q.dma_start(
                        out=k.AR2[:, ch * 512:(ch + 1) * 512],
                        in_=k.s_xbcT.ap()[ch][:, rg * 512:(rg + 1) * 512]), "qsx")
                k.dinc(q.dma_start(out=k.AR2[:, 16384:16896],
                                   in_=k.s_xbcT.ap()[32][:, rg * 512:(rg + 1) * 512]), "qsx")
                k.dinc(q.dma_start(out=k.dt_r[:], in_=k.s_dtT2.ap()[:, rg * 512:(rg + 1) * 512]), "qsx")
                k.dinc(q.dma_start(out=k.adt_r[:], in_=k.s_adtT.ap()[:, rg * 512:(rg + 1) * 512]), "qsx")

            for cc in range(4):
                self._ssd_chunk(rg, cc)

    def _ssd_chunk(self, rg, cc):
        k = self
        gc = rg * 4 + cc
        pse_bf = k.PSE[:].bitcast(BF16)
        s = dict(k.c)
        sp = k._sp
        d0, a0, p0, o0 = s.get("dve", 0), s.get("act", 0), s.get("pe", 0), s.get("pool", 0)
        qsx0 = s.get("qsx", 0)
        xs_tok = k.AR1[:, 0:4096]
        dtx = k.AR1[:, 4096:8192]
        dtxwb = k.AR1[:, 8192:12288]
        Lm = k.AR1[:, 12288:20480]
        Sm = k.AR1[:, 20480:28672]
        dxs = k.AR1[:, 28672:32768]
        v3 = lambda buf: buf.rearrange("p (h t) -> p h t", h=64)
        v64 = lambda buf: buf.rearrange("p (h e) -> p h e", h=64)
        btile = k.AR2[:, 16384 + cc * 128:16384 + (cc + 1) * 128]

        @k.block.vector
        def _(v):
            if sp is None:
                k.bwait(v, "vector")
            else:
                k.wait(v, "vector", "act", sp["a2"])
                k.wait(v, "vector", "pe", sp["p5"])
                k.wait(v, "vector", "qrg", sp["qrg"])
            k.wait(v, "vector", "qsx", k.c["qsx"])
            at = k.inc(v.tensor_tensor_scan(
                out=k.cs0[:], data0=k.adt_r[:, cc * 128:(cc + 1) * 128],
                data1=k.z64[:], initial=0.0, op0=ALU.add, op1=ALU.add), "dve")
            k.wait(v, "vector", "dve", at)
            k.inc(v.tensor_scalar(out=k.wb[:], in0=k.cs0[:],
                                  scalar1=k.cs0[:, 127:128], scalar2=None,
                                  op0=ALU.subtract), "dve")

        @k.block.scalar
        def _(a):
            k.bwait(a, "scalar")
            k.wait(a, "scalar", "dve", d0 + 1)
            k.inc(a.activation(out=k.uu[:], in_=k.cs0[:], func=AF.Exp), "act")
            k.wait(a, "scalar", "dve", d0 + 2)
            k.inc(a.activation(out=k.wb2[:], in_=k.wb[:], func=AF.Exp,
                               scale=k.kcol(-1.0, rows=64)), "act")

        @k.block.vector
        def _(v):
            k.wait(v, "vector", "act", a0 + 2)
            k.inc(v.tensor_tensor(out=k.dtwb[:], in0=k.dt_r[:, cc * 128:(cc + 1) * 128],
                                  in1=k.wb2[:], op=ALU.mult), "dve")

        @k.block.tensor
        def _(t):
            if sp is None:
                k.bwait(t, "tensor")
            else:
                k.wait(t, "tensor", "dve", sp["v3"])
                k.wait(t, "tensor", "dve", sp["v6"])
            srcs = None
            k.wait(t, "tensor", "dve", d0 + 1)
            k.inc(t.transpose(k.PSG[:, 0:64], k.cs0[:], k.c_idf[0:64, 0:64]), "pe")
            k.inc(t.transpose(k.PSH[:, 0:64], k.dt_r[:, cc * 128:(cc + 1) * 128],
                              k.c_idf[0:64, 0:64]), "pe")
            k.wait(t, "tensor", "dve", d0 + 4)
            k.wait(t, "tensor", "dve", d0 + 3)
            k.inc(t.transpose(k.PSG[:, 64:128], k.dtwb[:], k.c_idf[0:64, 0:64]), "pe")
            k.wait(t, "tensor", "dve", d0 + 5)
            k.wait(t, "tensor", "act", a0 + 1)
            k.inc(t.transpose(k.PSH[:, 64:128], k.uu[:], k.c_idf[0:64, 0:64]), "pe")
            k.wait(t, "tensor", "dve", d0 + 6)
            k.inc(t.transpose(k.PSG[0:1, 128:192], k.uu[:, 127:128],
                              k.c_idf[0:64, 0:64]), "pe")
            for b in range(4):
                if b >= 2:
                    k.wait(t, "tensor", "dve", d0 + 8 + (b - 1))
                for j in range(8):
                    chn = b * 8 + j
                    half = (b % 2) * 1024
                    src = k.AR2[:, chn * 512 + cc * 128:chn * 512 + cc * 128 + 128]
                    ins = t.transpose(pse_bf[:, half + j * 128:half + (j + 1) * 128],
                                      src, k.c_idbf[:])
                    if j == 7:
                        k.inc(ins, "pe")
            k.wait(t, "tensor", "dve", d0 + 11)
            k.inc(t.transpose(pse_bf[:, 0:64], btile[0:64, :], k.c_idbf[0:64, 0:64]), "pe")

        @k.block.vector
        def _(v):
            k.wait(v, "vector", "pe", p0 + 1)
            k.inc(v.tensor_copy(k.cstok[:], k.PSG[:, 0:64]), "dve")
            k.wait(v, "vector", "pe", p0 + 2)
            k.inc(v.tensor_copy(k.dttok[:], k.PSH[:, 0:64]), "dve")
            k.wait(v, "vector", "pe", p0 + 3)
            k.inc(v.tensor_copy(k.dwtok[:], k.PSG[:, 64:128]), "dve")
            k.wait(v, "vector", "pe", p0 + 4)
            k.inc(v.tensor_copy(k.utok[:], k.PSH[:, 64:128]), "dve")
            k.wait(v, "vector", "pe", p0 + 5)
            k.inc(v.tensor_copy(k.rowsb[0:1, 0:64], k.PSG[0:1, 128:192]), "dve")
            for b in range(4):
                k.wait(v, "vector", "pe", p0 + 6 + b)
                half = (b % 2) * 1024
                k.inc(v.tensor_copy(xs_tok[:, b * 1024:(b + 1) * 1024],
                                    pse_bf[:, half:half + 1024]), "dve")
            k.wait(v, "vector", "pe", p0 + 10)
            k.inc(v.tensor_copy(k.btok[:], pse_bf[:, 0:64]), "dve")
            k.wait(v, "vector", "dve", d0 + 13)
            k.inc(v.tensor_tensor(out=v64(dtx), in0=v64(xs_tok),
                                  in1=k.dttok[:].unsqueeze(2).broadcast_to([128, 64, 64]),
                                  op=ALU.mult), "dve")
            k.inc(v.tensor_tensor(out=v64(dxs), in0=v64(xs_tok),
                                  in1=k.p_drep[:].unsqueeze(2).broadcast_to([128, 64, 64]),
                                  op=ALU.mult), "dve")
            k.inc(v.tensor_tensor(out=v64(dtxwb), in0=v64(xs_tok),
                                  in1=k.dwtok[:].unsqueeze(2).broadcast_to([128, 64, 64]),
                                  op=ALU.mult), "dve")

        @k.block.gpsimd
        def _(g):
            if sp is None:
                k.bwait(g, "gpsimd")
            else:
                k.wait(g, "gpsimd", "pe", sp["pe"])
            k.wait(g, "gpsimd", "qsx", k.c["qsx"])
            k.inc(g.tensor_copy(k.ccb[:], btile[64:128, :]), "pool")
            k.wait(g, "gpsimd", "dve", d0 + 1)
            k.dinc(g.dma_start(
                out=k.s_rowA.ap().rearrange("o (h t) -> (o h) t", h=64),
                in_=k.cs0[:]), "qrg")
            k.wait(g, "gpsimd", "dve", d0 + 8)
            k.dinc(g.dma_start(out=k.s_rowB.ap()[0:1, 0:64],
                               in_=k.rowsb[0:1, 0:64]), "qrg")
            k.wait(g, "gpsimd", "qrg", k.c["qrg"])
            if sp is not None:
                k.wait(g, "gpsimd", "act", sp["a3"])
                k.wait(g, "gpsimd", "dve", sp["v6"])
            k.dinc(g.dma_start(out=k.AF1[:],
                               in_=bass.AP(k.s_rowA, 0, [[0, 128], [1, 8192]])), "qrb")
            k.dinc(g.dma_start(out=k.uerep[:],
                               in_=bass.AP(k.s_rowB, 0, [[0, 64], [1, 64]])), "qrb")

        qrb_v = k.c["qrb"]

        @k.block.vector
        def _(v):
            k.wait(v, "vector", "qrb", qrb_v)
            k.inc(v.tensor_tensor(
                out=v3(k.AF1[:]), in0=v3(k.AF1[:]),
                in1=k.cstok[:].unsqueeze(2).broadcast_to([128, 64, 128]),
                op=ALU.subtract), "dve")

        @k.block.gpsimd
        def _(g):
            k.wait(g, "gpsimd", "dve", d0 + 17)
            k.inc(g.tensor_tensor(
                out=v3(k.AF1[:]), in0=v3(k.AF1[:]),
                in1=k.c_smlog[:].unsqueeze(1).broadcast_to([128, 64, 128]),
                op=ALU.add), "pool")

        @k.block.scalar
        def _(a):
            k.wait(a, "scalar", "pool", o0 + 2)
            if sp is not None:
                k.wait(a, "scalar", "pool", sp["o3"])
            k.inc(a.activation(out=Lm, in_=k.AF1[:], func=AF.Exp), "act")

        @k.block.tensor
        def _(t):
            k.wait(t, "tensor", "pool", o0 + 1)
            if sp is not None:
                k.wait(t, "tensor", "dve", sp["v5"])
            k.inc(t.matmul(k.PSH[:, 128:256], btile[0:64, :], k.ccb[:],
                           start=True, stop=True), "pe")

        @k.block.vector
        def _(v):
            k.wait(v, "vector", "pe", p0 + 11)
            if sp is not None:
                k.wait(v, "vector", "pool", sp["o3"])
            k.inc(v.tensor_copy(k.sraw[:], k.PSH[:, 128:256]), "dve")

        @k.block.gpsimd
        def _(g):
            k.wait(g, "gpsimd", "act", a0 + 3)
            k.wait(g, "gpsimd", "dve", d0 + 18)
            k.inc(g.tensor_tensor(
                out=v3(Sm), in0=v3(Lm),
                in1=k.sraw[:].unsqueeze(1).broadcast_to([128, 64, 128]),
                op=ALU.mult), "pool")

        @k.block.tensor
        def _(t):
            k.wait(t, "tensor", "dve", d0 + 18)
            k.wait(t, "tensor", "pool", o0 + 3)
            if sp is not None:
                k.wait(t, "tensor", "dve", sp["v6"])
            for g in range(8):
                pd = [k.PSA, k.PSB][g % 2]
                po = [k.PSC, k.PSF][g % 2]
                pst = k.PSE[0:64, (g % 2) * 512:(g % 2) * 512 + 512]
                if g >= 2:
                    k.wait(t, "tensor", "dve", d0 + 18 + 3 * (g - 1))
                k.inc(t.matmul(po[:], k.ccb[:], k.ST_BF[:, g * 512:(g + 1) * 512],
                               start=True, stop=True), "pe")
                ins = t.matmul(pd[:], k.c_idbf[:], dxs[:, g * 512:(g + 1) * 512],
                               start=True, stop=False)
                for hh in range(8):
                    h = g * 8 + hh
                    ins = t.matmul(pd[:, hh * 64:(hh + 1) * 64],
                                   Sm[:, h * 128:(h + 1) * 128],
                                   dtx[:, h * 64:(h + 1) * 64],
                                   start=False, stop=(hh == 7))
                k.inc(ins, "pe")
                k.inc(t.matmul(pst, k.btok[:], dtxwb[:, g * 512:(g + 1) * 512],
                               start=True, stop=True), "pe")

        @k.block.vector
        def _(v):
            for g in range(8):
                pd = [k.PSA, k.PSB][g % 2]
                po = [k.PSC, k.PSF][g % 2]
                pst = k.PSE[0:64, (g % 2) * 512:(g % 2) * 512 + 512]
                stage = [k.YTMP2, k.YTMP3][g % 2]
                k.wait(v, "vector", "pe", p0 + 11 + 3 * g + 2)
                k.wait(v, "vector", "qys%d" % (g % 2),
                       s.get("qys%d" % (g % 2), 0) + 16 * (g // 2))
                k.wait(v, "vector", "dve", k.c["dve"])
                at = k.inc(v.tensor_tensor(
                    out=k.YTMP[:].rearrange("p (h e) -> p h e", h=8),
                    in0=po[:].rearrange("p (h e) -> p h e", h=8),
                    in1=k.utok[:, rg * 0 + 8 * g:8 * g + 8].unsqueeze(2).broadcast_to([128, 8, 64]),
                    op=ALU.mult), "dve")
                k.wait(v, "vector", "dve", at)
                k.inc(v.tensor_tensor(out=stage[:], in0=k.YTMP[:], in1=pd[:],
                                      op=ALU.add), "dve")
                k.wait(v, "vector", "pe", p0 + 11 + 3 * g + 3)
                k.inc(v.tensor_tensor(out=k.ST_S[:, g * 512:(g + 1) * 512],
                                      in0=pst, in1=k.ST_S[:, g * 512:(g + 1) * 512],
                                      op=ALU.add), "dve")
            k.wait(v, "vector", "qrb", qrb_v)
            k.wait(v, "vector", "dve", k.c["dve"])
            at = k.inc(v.tensor_tensor(
                out=k.ST_S[:].rearrange("p (h e) -> p h e", h=64),
                in0=k.ST_S[:].rearrange("p (h e) -> p h e", h=64),
                in1=k.uerep[:].unsqueeze(2).broadcast_to([64, 64, 64]),
                op=ALU.mult), "dve")
            k.wait(v, "vector", "dve", at)
            k.inc(v.tensor_copy(k.ST_BF[:], k.ST_S[:]), "dve")

        @k.block.gpsimd
        def _(g_):
            for g in range(8):
                stage = [k.YTMP2, k.YTMP3][g % 2]
                k.wait(g_, "gpsimd", "dve", d0 + 18 + 3 * g + 2)
                k.dinc(g_.dma_start(
                    out=k.s_y.ap()[gc][:, g * 512:(g + 1) * 512], in_=stage[:]),
                    "qys%d" % (g % 2))

        k._sp = {
            "a2": a0 + 2, "a3": a0 + 3, "p5": p0 + 5, "pe": k.c["pe"],
            "v3": d0 + 16, "v5": d0 + 18, "v6": k.c["dve"], "o3": o0 + 3,
            "pool": k.c["pool"], "qrg": k.c["qrg"],
        }

    def emit_m5(self, x_src, x_dst):
        k = self
        L = k.L
        pse_bf = k.PSE[:].bitcast(BF16)
        PS8 = [k.PSA, k.PSB, k.PSC, k.PSF, k.PSG, k.PSH]
        YGT = lambda: k.AR1[:, 0:16384]
        nwbf = k.AR1[:, 28672:32768]

        k.barrier()

        @k.block.sync
        def _(q):
            k.bwait(q, "sync")
            k.dinc(q.dma_start(out=k.AF1[:, 0:4096], in_=k.nwbc.ap()), "qnw")

        qv = k.c["qnw"]

        @k.block.vector
        def _(v):
            k.bwait(v, "vector")
            k.wait(v, "vector", "qnw", qv)
            k.inc(v.tensor_copy(nwbf, k.AF1[:, 0:4096]), "dve")

        for rg in range(L // 512):
            k.barrier()

            @k.block.sync
            def _(q):
                k.bwait(q, "sync")
                for j in range(32):
                    k.dinc(q.dma_start(
                        out=k.AR2[:, j * 512:(j + 1) * 512],
                        in_=k.s_projT.ap()[j][:, rg * 512:(rg + 1) * 512]), "qsz")

            for tb in range(4):
                s = k.barrier()
                d0, a0, p0 = s.get("dve", 0), s.get("act", 0), s.get("pe", 0)
                qz0 = s.get("qsz", 0)
                z_tok = k.AR1[:, 16384:20480]
                zs = k.AR1[:, 20480:24576]
                scrap = k.AR1[:, 24576:28672]
                y = k.AF1[:, 0:4096]
                yg1 = k.AF1[:, 4096:8192]

                @k.block.sync
                def _(q):
                    k.bwait(q, "sync")
                    k.dinc(q.dma_start(out=y, in_=k.s_y.ap()[rg * 4 + tb]), "qyl")

                qyv = k.c["qyl"]

                @k.block.tensor
                def _(t):
                    k.bwait(t, "tensor")
                    for b in range(2):
                        if b == 1:
                            k.wait(t, "tensor", "dve", d0 + 1)
                        for j in range(16):
                            jj = b * 16 + j
                            src = k.AR2[:, jj * 512 + tb * 128:jj * 512 + tb * 128 + 128]
                            ins = t.transpose(pse_bf[:, j * 128:(j + 1) * 128],
                                              src, k.c_idbf[:])
                            if j == 15:
                                k.inc(ins, "pe")

                @k.block.vector
                def _(v):
                    k.bwait(v, "vector")
                    for b in range(2):
                        k.wait(v, "vector", "pe", p0 + 1 + b)
                        k.inc(v.tensor_copy(z_tok[:, b * 2048:(b + 1) * 2048],
                                            pse_bf[:, 0:2048]), "dve")

                @k.block.scalar
                def _(a):
                    k.bwait(a, "scalar")
                    k.wait(a, "scalar", "dve", d0 + 2)
                    k.inc(a.activation(out=zs, in_=z_tok, func=AF.Silu), "act")
                    k.wait(a, "scalar", "qyl", qyv)
                    k.inc(a.activation(out=scrap, in_=y, func=AF.Square,
                                       accum_out=k.SM1[:, 0:1]), "act")

                @k.block.vector
                def _(v):
                    k.wait(v, "vector", "act", a0 + 2)
                    k.inc(v.tensor_scalar_mul(k.SM1[:, 1:2], k.SM1[:, 0:1],
                                              1.0 / INTER), "dve")

                @k.block.scalar
                def _(a):
                    k.wait(a, "scalar", "dve", d0 + 3)
                    at = k.inc(a.activation(out=k.SM1[:, 2:3], in_=k.SM1[:, 1:2],
                                            func=AF.Ln, bias=k.kcol(GN_EPS)), "act")
                    k.wait(a, "scalar", "act", at)
                    k.inc(a.activation(out=k.SM1[:, 3:4], in_=k.SM1[:, 2:3],
                                       func=AF.Exp, scale=k.kcol(-0.5)), "act")

                @k.block.vector
                def _(v):
                    k.wait(v, "vector", "act", a0 + 4)
                    at = k.inc(v.scalar_tensor_tensor(
                        out=yg1, in0=y, scalar=k.SM1[:, 3:4], in1=nwbf,
                        op0=ALU.mult, op1=ALU.mult), "dve")
                    k.wait(v, "vector", "dve", at)
                    k.inc(v.tensor_tensor(out=scrap, in0=yg1, in1=zs,
                                          op=ALU.mult), "dve")

                @k.block.tensor
                def _(t):
                    k.wait(t, "tensor", "dve", d0 + 5)
                    for b in range(2):
                        if b == 1:
                            k.wait(t, "tensor", "dve", d0 + 6)
                        for j in range(16):
                            jj = b * 16 + j
                            ins = t.transpose(pse_bf[:, j * 128:(j + 1) * 128],
                                              scrap[:, jj * 128:(jj + 1) * 128],
                                              k.c_idbf[:])
                            if j == 15:
                                k.inc(ins, "pe")

                @k.block.vector
                def _(v):
                    for b in range(2):
                        k.wait(v, "vector", "pe", p0 + 3 + b)
                        st = YGT().rearrange("p (i f) -> p i f", f=512)
                        k.inc(v.tensor_copy(
                            st[:, b * 16:(b + 1) * 16, tb * 128:tb * 128 + 128],
                            pse_bf[:, 0:2048].rearrange("p (i f) -> p i f", f=128)),
                            "dve")

            s = k.barrier()
            d0, p0 = s.get("dve", 0), s.get("pe", 0)
            w0s = [s.get("qwo0", 0), s.get("qwo1", 0)]
            x0s = [s.get("qxr0", 0), s.get("qxr1", 0)]
            m0s = [s.get("qms0", 0), s.get("qms1", 0)]
            PSL = PS8 + [k.PSE[:, 0:512], k.PSE[:, 512:1024]]

            @k.block.scalar
            def _(a):
                k.bwait(a, "scalar")
                for half in range(2):
                    for i in range(32):
                        ii = half * 32 + i
                        if ii >= 2:
                            k.wait(a, "scalar", "pe", p0 + (ii - 1))
                        ws = k.AR2[:, (ii % 2) * 2048:(ii % 2) * 2048 + 2048]
                        k.dinc(a.dma_start(out=ws, in_=k.s_woutT.ap()[i]),
                               "qwo%d" % (ii % 2))

            @k.block.tensor
            def _(t):
                k.bwait(t, "tensor")
                for half in range(2):
                    for i in range(32):
                        ii = half * 32 + i
                        k.wait(t, "tensor", "qwo%d" % (ii % 2),
                               w0s[ii % 2] + 16 * (ii // 2 + 1))
                        if i == 0 and half == 1:
                            k.wait(t, "tensor", "dve", d0 + 8)
                        ws = k.AR2[:, (ii % 2) * 2048:(ii % 2) * 2048 + 2048]
                        for t2 in range(2):
                            for dj in range(4):
                                ins = t.matmul(
                                    PSL[t2 * 4 + dj][:],
                                    YGT()[:, i * 512 + (half * 2 + t2) * 128:
                                          i * 512 + (half * 2 + t2) * 128 + 128],
                                    ws[:, dj * 512:(dj + 1) * 512],
                                    start=(i == 0), stop=(i == 31))
                        k.inc(ins, "pe")

            @k.block.sync
            def _(q):
                k.bwait(q, "sync")
                for half in range(2):
                    for t2 in range(2):
                        tb_abs = half * 2 + t2
                        xt = (k.XT if tb_abs % 2 == 0 else k.AF2)[:, 0:2048]
                        if tb_abs >= 2:
                            k.wait(q, "sync", "dve", d0 + 4 * (tb_abs - 1))
                        row = (rg * 4 + tb_abs) * 128
                        k.dinc(q.dma_start(out=xt, in_=x_src.ap()[row:row + 128, :]),
                               "qxr%d" % (tb_abs % 2))

            @k.block.vector
            def _(v):
                k.bwait(v, "vector")
                for half in range(2):
                    k.wait(v, "vector", "pe", p0 + 32 * (half + 1))
                    for t2 in range(2):
                        tb_abs = half * 2 + t2
                        xt = (k.XT if tb_abs % 2 == 0 else k.AF2)[:, 0:2048]
                        stage = k.AF1[:, (tb_abs % 2) * 2048:(tb_abs % 2) * 2048 + 2048]
                        k.wait(v, "vector", "qxr%d" % (tb_abs % 2),
                               x0s[tb_abs % 2] + 16 * (tb_abs // 2 + 1))
                        if tb_abs >= 2:
                            k.wait(v, "vector", "qms%d" % (tb_abs % 2),
                                   m0s[tb_abs % 2] + 16 * (tb_abs // 2))
                        k.wait(v, "vector", "dve", k.c["dve"])
                        for dj in range(4):
                            ins = v.tensor_tensor(
                                out=stage[:, dj * 512:(dj + 1) * 512],
                                in0=PSL[t2 * 4 + dj][:], in1=xt[:, dj * 512:(dj + 1) * 512],
                                op=ALU.add)
                        k.inc(ins, "dve", 4)

            @k.block.gpsimd
            def _(g):
                k.bwait(g, "gpsimd")
                for half in range(2):
                    for t2 in range(2):
                        tb_abs = half * 2 + t2
                        stage = k.AF1[:, (tb_abs % 2) * 2048:(tb_abs % 2) * 2048 + 2048]
                        k.wait(g, "gpsimd", "dve", d0 + 4 * (tb_abs + 1))
                        row = (rg * 4 + tb_abs) * 128
                        k.dinc(g.dma_start(out=x_dst.ap()[row:row + 128, :],
                                           in_=stage), "qms%d" % (tb_abs % 2))

    def emit_ffn(self, x_src, x_dst):
        k = self
        L = k.L
        for rg in range(L // 512):
            s = k.barrier()
            d0, p0, a0 = s.get("dve", 0), s.get("pe", 0), s.get("act", 0)
            w0s = [s.get("qws0", 0), s.get("qws1", 0)]
            u0s = [s.get("qwu0", 0), s.get("qwu1", 0)]
            x0s = [s.get("qxr0", 0), s.get("qxr1", 0)]
            m0s = [s.get("qms0", 0), s.get("qms1", 0)]
            l0 = s.get("qlnr", 0)

            @k.block.sync
            def _(q):
                k.bwait(q, "sync")
                for c in range(DC):
                    k.dinc(q.dma_start(
                        out=k.AR2[:, c * 512:(c + 1) * 512],
                        in_=k.s_lnT.ap()[c][:, rg * 512:(rg + 1) * 512]), "qlnr")
                for tb in range(4):
                    xt = (k.XT if tb % 2 == 0 else k.AF2)[:, 0:2048]
                    if tb >= 2:
                        k.wait(q, "sync", "dve", d0 + 160 + (tb - 1))
                    row = (rg * 4 + tb) * 128
                    k.dinc(q.dma_start(out=xt, in_=x_src.ap()[row:row + 128, :]),
                           "qxr%d" % (tb % 2))

            @k.block.scalar
            def _(a):
                k.bwait(a, "scalar")
                for hj in range(64):
                    if hj >= 2:
                        k.wait(a, "scalar", "pe", p0 + hj - 1)
                    ws = k.AR2[:, 8192 + (hj % 2) * 2048:8192 + (hj % 2) * 2048 + 2048]
                    k.dinc(a.dma_start(out=ws, in_=k.s_wupT.ap()[hj]), "qws%d" % (hj % 2))

            @k.block.tensor
            def _(t):
                k.bwait(t, "tensor")
                k.wait(t, "tensor", "qlnr", l0 + 16 * DC)
                for hj in range(64):
                    k.wait(t, "tensor", "qws%d" % (hj % 2), w0s[hj % 2] + 16 * (hj // 2 + 1))
                    if hj >= 3:
                        k.wait(t, "tensor", "dve", d0 + hj - 2)
                    ws = k.AR2[:, 8192 + (hj % 2) * 2048:8192 + (hj % 2) * 2048 + 2048]
                    ps = [k.PSA, k.PSB, k.PSC][hj % 3]
                    ins = None
                    for c in range(DC):
                        ins = t.matmul(ps[:], ws[:, c * 128:(c + 1) * 128],
                                       k.AR2[:, c * 512:(c + 1) * 512],
                                       start=(c == 0), stop=(c == DC - 1))
                    k.inc(ins, "pe")

            @k.block.vector
            def _(v):
                k.bwait(v, "vector")
                for hj in range(64):
                    k.wait(v, "vector", "pe", p0 + hj + 1)
                    ps = [k.PSA, k.PSB, k.PSC][hj % 3]
                    k.inc(v.tensor_scalar(
                        out=k.AR1[:, hj * 512:(hj + 1) * 512], in0=ps[:],
                        scalar1=k.p_upsc[:, hj:hj + 1], scalar2=0.0,
                        op0=ALU.mult, op1=ALU.max), "dve")

            dn_p0 = p0 + 64

            @k.block.gpsimd
            def _(g):
                k.bwait(g, "gpsimd")
                for j in range(16):
                    for qq in range(4):
                        iq = j * 4 + qq
                        if iq >= 2:
                            pj, pq = (iq - 2) // 4, (iq - 2) % 4
                            k.wait(g, "gpsimd", "pe", dn_p0 + 8 * pj + pq + 1)
                        ws = k.AR2[:, 12288 + (iq % 2) * 2048:12288 + (iq % 2) * 2048 + 2048]
                        k.dinc(g.dma_start(
                            out=ws, in_=k.s_wdnT.ap()[j][:, qq * 2048:(qq + 1) * 2048]),
                            "qwu%d" % (iq % 2))

            @k.block.tensor
            def _(t):
                k.wait(t, "tensor", "dve", d0 + 64)
                for j in range(16):
                    ps = [k.PSF, k.PSG][j % 2]
                    if j >= 2:
                        k.wait(t, "tensor", "dve", d0 + 64 + 6 * (j - 1))
                    for qq in range(4):
                        iq = j * 4 + qq
                        k.wait(t, "tensor", "qwu%d" % (iq % 2),
                               u0s[iq % 2] + 16 * (iq // 2 + 1))
                        ws = k.AR2[:, 12288 + (iq % 2) * 2048:12288 + (iq % 2) * 2048 + 2048]
                        ins = None
                        for hh in range(16):
                            hchunk = qq * 16 + hh
                            ins = t.matmul(ps[:], ws[:, hh * 128:(hh + 1) * 128],
                                           k.AR1[:, hchunk * 512:(hchunk + 1) * 512],
                                           start=(iq % 4 == 0 and hh == 0),
                                           stop=(qq == 3 and hh == 15))
                        k.inc(ins, "pe")
                    k.wait(t, "tensor", "dve", d0 + 64 + 6 * j + 1)
                    if j >= 1:
                        k.wait(t, "tensor", "dve", d0 + 64 + 6 * j)
                    ytd = [k.YTMP, k.YTMP2][j % 2]
                    for t4 in range(4):
                        k.inc(t.transpose(k.PSH[:, t4 * 128:(t4 + 1) * 128],
                                          ytd[:, t4 * 128:(t4 + 1) * 128],
                                          k.c_idf[:]), "pe")

            @k.block.vector
            def _(v):
                for j in range(16):
                    ps = [k.PSF, k.PSG][j % 2]
                    ytd = [k.YTMP, k.YTMP2][j % 2]
                    k.wait(v, "vector", "pe", dn_p0 + 8 * j + 4)
                    k.wait(v, "vector", "dve", k.c["dve"])
                    k.inc(v.tensor_scalar(
                        out=ytd[:], in0=ps[:], scalar1=k.p_dscc[:, j:j + 1],
                        scalar2=None, op0=ALU.mult), "dve")
                    k.wait(v, "vector", "pe", dn_p0 + 8 * j + 8)
                    st = k.AF1[:].rearrange("p (tb f) -> p tb f", f=2048)
                    k.inc(v.tensor_copy(
                        st[:, :, j * 128:(j + 1) * 128],
                        k.PSH[:].rearrange("p (tb f) -> p tb f", f=128)), "dve", 5)

            @k.block.vector
            def _(v):
                for tb in range(4):
                    xt = (k.XT if tb % 2 == 0 else k.AF2)[:, 0:2048]
                    stage = k.AF1[:, tb * 2048:(tb + 1) * 2048]
                    k.wait(v, "vector", "qxr%d" % (tb % 2), x0s[tb % 2] + 16 * (tb // 2 + 1))
                    k.wait(v, "vector", "dve", d0 + 64 + 96)
                    if tb >= 2:
                        k.wait(v, "vector", "qms%d" % (tb % 2), m0s[tb % 2] + 16 * (tb // 2))
                    k.inc(v.tensor_tensor(out=stage, in0=stage, in1=xt, op=ALU.add),
                          "dve")

            @k.block.gpsimd
            def _(g):
                for tb in range(4):
                    stage = k.AF1[:, tb * 2048:(tb + 1) * 2048]
                    k.wait(g, "gpsimd", "dve", d0 + 64 + 96 + tb + 1)
                    row = (rg * 4 + tb) * 128
                    k.dinc(g.dma_start(out=x_dst.ap()[row:row + 128, :], in_=stage),
                           "qms%d" % (tb % 2))

    def emit_attn(self, x_src, x_dst):
        k = self
        L = k.L
        pse_bf = k.PSE[:].bitcast(BF16)

        for rg in range(L // 512):
            s = k.barrier()
            d0, p0 = s.get("dve", 0), s.get("pe", 0)
            w0s = [s.get("qws0", 0), s.get("qws1", 0)]
            st0s = [s.get("qst0", 0), s.get("qst1", 0), s.get("qst2", 0)]
            l0 = s.get("qlnr", 0)

            @k.block.sync
            def _(q):
                k.bwait(q, "sync")
                for c in range(DC):
                    k.dinc(q.dma_start(
                        out=k.AR2[:, c * 512:(c + 1) * 512],
                        in_=k.s_lnT.ap()[c][:, rg * 512:(rg + 1) * 512]), "qlnr")

            @k.block.scalar
            def _(a):
                k.bwait(a, "scalar")
                for j in range(48):
                    if j >= 2:
                        k.wait(a, "scalar", "pe", p0 + j - 1)
                    ws = k.AR2[:, 8192 + (j % 2) * 2048:8192 + (j % 2) * 2048 + 2048]
                    k.dinc(a.dma_start(out=ws, in_=k.s_wqkv.ap()[j]), "qws%d" % (j % 2))

            @k.block.tensor
            def _(t):
                k.bwait(t, "tensor")
                k.wait(t, "tensor", "qlnr", l0 + 16 * DC)
                for j in range(48):
                    k.wait(t, "tensor", "qws%d" % (j % 2), w0s[j % 2] + 16 * (j // 2 + 1))
                    if j >= 3:
                        k.wait(t, "tensor", "dve", d0 + j - 2)
                    ws = k.AR2[:, 8192 + (j % 2) * 2048:8192 + (j % 2) * 2048 + 2048]
                    ps = [k.PSA, k.PSB, k.PSC][j % 3]
                    ins = None
                    for c in range(DC):
                        ins = t.matmul(ps[:], ws[:, c * 128:(c + 1) * 128],
                                       k.AR2[:, c * 512:(c + 1) * 512],
                                       start=(c == 0), stop=(c == DC - 1))
                    k.inc(ins, "pe")

            @k.block.vector
            def _(v):
                k.bwait(v, "vector")
                for j in range(48):
                    k.wait(v, "vector", "pe", p0 + j + 1)
                    if j >= 3:
                        sl = j % 3
                        k.wait(v, "vector", "qst%d" % sl,
                               st0s[sl] + 16 * ((j - 3) // 3 + 1))
                    ps = [k.PSA, k.PSB, k.PSC][j % 3]
                    stg = k.AR1[:, (j % 3) * 512:(j % 3) * 512 + 512]
                    if j < 16:
                        k.inc(v.tensor_scalar(out=stg, in0=ps[:],
                                              scalar1=k.p_bqkv[:, j:j + 1],
                                              scalar2=ASCALE,
                                              op0=ALU.add, op1=ALU.mult), "dve")
                    else:
                        k.inc(v.tensor_scalar_add(stg, ps[:], k.p_bqkv[:, j:j + 1]), "dve")

            @k.block.gpsimd
            def _(g):
                k.bwait(g, "gpsimd")
                for j in range(48):
                    k.wait(g, "gpsimd", "dve", d0 + j + 1)
                    stg = k.AR1[:, (j % 3) * 512:(j % 3) * 512 + 512]
                    k.dinc(g.dma_start(
                        out=k.s_qkvT.ap()[j][:, rg * 512:(rg + 1) * 512], in_=stg),
                        "qst%d" % (j % 3))

        for h in range(AH):
            self._attn_head(h)

        k.barrier()

        @k.block.sync
        def _(q):
            k.bwait(q, "sync")
            for i in range(16):
                k.dinc(q.dma_start(out=k.AR1[:, i * 2048:(i + 1) * 2048],
                                   in_=k.s_wouta.ap()[i]), "qoaw")

        qoaw_v = k.c["qoaw"]
        for rg in range(L // 512):
            s = k.barrier()
            d0, p0, o0 = s.get("dve", 0), s.get("pe", 0), s.get("pool", 0)
            a0s = s.get("qaor", 0)
            x0s = [s.get("qxr0", 0), s.get("qxr1", 0)]
            m0s = [s.get("qms0", 0), s.get("qms1", 0)]

            @k.block.sync
            def _(q):
                k.bwait(q, "sync")
                k.wait(q, "sync", "qoaw", qoaw_v)
                for i in range(16):
                    k.dinc(q.dma_start(
                        out=k.AR2[:, i * 512:(i + 1) * 512],
                        in_=k.s_aoT.ap()[i][:, rg * 512:(rg + 1) * 512]), "qaor")
                for tb in range(4):
                    xt = (k.XT if tb % 2 == 0 else k.AF2)[:, 0:2048]
                    if tb >= 2:
                        k.wait(q, "sync", "pool", o0 + (tb - 1))
                    row = (rg * 4 + tb) * 128
                    k.dinc(q.dma_start(out=xt, in_=x_src.ap()[row:row + 128, :]),
                           "qxr%d" % (tb % 2))

            @k.block.gpsimd
            def _(g):
                k.bwait(g, "gpsimd")
                for tb in range(4):
                    xt = (k.XT if tb % 2 == 0 else k.AF2)[:, 0:2048]
                    xb = k.AF1[:, 4096 + (tb % 2) * 2048:4096 + (tb % 2) * 2048 + 2048]
                    k.wait(g, "gpsimd", "qxr%d" % (tb % 2), x0s[tb % 2] + 16 * (tb // 2 + 1))
                    if tb >= 2:
                        k.wait(g, "gpsimd", "dve", d0 + 4 * (tb - 1))
                    k.inc(g.tensor_tensor(out=xb, in0=xt, in1=k.BOUT[:], op=ALU.add),
                          "pool")

            @k.block.tensor
            def _(t):
                k.bwait(t, "tensor")
                k.wait(t, "tensor", "qaor", a0s + 16 * 16)
                for tb in range(4):
                    if tb >= 2:
                        k.wait(t, "tensor", "dve", d0 + 4 * (tb - 1))
                    ins = None
                    for i in range(16):
                        for dj in range(4):
                            pslist = ([k.PSA[:], k.PSB[:], k.PSC[:], k.PSF[:]]
                                      if tb % 2 == 0 else
                                      [k.PSG[:], k.PSH[:], k.PSE[:, 0:512], k.PSE[:, 512:1024]])
                            ins = t.matmul(
                                pslist[dj],
                                k.AR2[:, i * 512 + tb * 128:i * 512 + tb * 128 + 128],
                                k.AR1[:, i * 2048 + dj * 512:i * 2048 + (dj + 1) * 512],
                                start=(i == 0), stop=(i == 15))
                    k.inc(ins, "pe")

            @k.block.vector
            def _(v):
                k.bwait(v, "vector")
                for tb in range(4):
                    xb = k.AF1[:, 4096 + (tb % 2) * 2048:4096 + (tb % 2) * 2048 + 2048]
                    stage = k.AF1[:, (tb % 2) * 2048:(tb % 2) * 2048 + 2048]
                    k.wait(v, "vector", "pe", p0 + tb + 1)
                    k.wait(v, "vector", "pool", o0 + tb + 1)
                    if tb >= 2:
                        k.wait(v, "vector", "qms%d" % (tb % 2), m0s[tb % 2] + 16 * (tb // 2))
                    PSLt = ([k.PSA[:], k.PSB[:], k.PSC[:], k.PSF[:]] if tb % 2 == 0
                            else [k.PSG[:], k.PSH[:], k.PSE[:, 0:512], k.PSE[:, 512:1024]])
                    ins = None
                    for dj in range(4):
                        ins = v.tensor_tensor(out=stage[:, dj * 512:(dj + 1) * 512],
                                              in0=PSLt[dj],
                                              in1=xb[:, dj * 512:(dj + 1) * 512],
                                              op=ALU.add)
                    k.inc(ins, "dve", 4)

            @k.block.gpsimd
            def _(g):
                for tb in range(4):
                    stage = k.AF1[:, (tb % 2) * 2048:(tb % 2) * 2048 + 2048]
                    k.wait(g, "gpsimd", "dve", d0 + 4 * (tb + 1))
                    row = (rg * 4 + tb) * 128
                    k.dinc(g.dma_start(out=x_dst.ap()[row:row + 128, :], in_=stage),
                           "qms%d" % (tb % 2))

    def _attn_head(self, h):
        k = self
        L = k.L
        NQB = L // 128
        NKB = L // 128
        NBLK = L // 512
        pse_bf = k.PSE[:].bitcast(BF16)
        QT = k.AR2[:, 0:4096][:, 0:L]
        KT = k.AR2[:, 4096:4096 + L]
        VT = k.AR2[:, 8192:8192 + L]
        Vtok = k.AR2[:, 12288:12288 + L]
        scrap = k.AR2[:, 16384:17408]
        pe_head0 = k.c.get("pe", 0)
        qbn0 = k.c.get("qbn", 0)
        qao_now = {s_: k.c.get(s_, 0) for s_ in ("qao0", "qao1")}
        if h == 0:
            s = k.barrier()

        @k.block.sync
        def _(q):
            if h == 0:
                k.bwait(q, "sync")
            else:
                k.wait(q, "sync", "pe", pe_head0)
            k.dinc(q.dma_start(out=QT, in_=k.s_qkvT.ap()[h][:, 0:L]), "qhd")
            k.dinc(q.dma_start(out=KT, in_=k.s_qkvT.ap()[16 + h][:, 0:L]), "qhd")
            k.dinc(q.dma_start(out=VT, in_=k.s_qkvT.ap()[32 + h][:, 0:L]), "qhd")

        qhd_v = k.c["qhd"]

        nbat = (NKB + 7) // 8
        k._vt_evac = {}
        for b in range(nbat):
            n_t = min(8, NKB - b * 8)

            @k.block.tensor
            def _(t):
                if b == 0:
                    if h == 0:
                        k.bwait(t, "tensor")
                    k.wait(t, "tensor", "qhd", qhd_v)
                    if hasattr(k, "_r4_evac"):
                        k.wait(t, "tensor", "dve", k._r4_evac)
                    if hasattr(k, "_srow_evac"):
                        k.wait(t, "tensor", "dve", k._srow_evac)
                if b >= 2:
                    k.wait(t, "tensor", "dve", k._vt_evac[b - 2])
                half = (b % 2) * 1024
                ins = None
                for j in range(n_t):
                    kb = b * 8 + j
                    ins = t.transpose(pse_bf[:, half + j * 128:half + (j + 1) * 128],
                                      VT[:, kb * 128:(kb + 1) * 128], k.c_idbf[:])
                k.inc(ins, "pe")

            pe_at = k.c["pe"]

            @k.block.vector
            def _(v):
                if b == 0 and h == 0:
                    k.bwait(v, "vector")
                k.wait(v, "vector", "pe", pe_at)
                half = (b % 2) * 1024
                at = k.inc(v.tensor_copy(Vtok[:, b * 1024:b * 1024 + n_t * 128],
                                         pse_bf[:, half:half + n_t * 128]), "dve")
                k._vt_evac[b] = at

        k._vt_last = k._vt_evac[max(k._vt_evac)]

        for blk in range(NBLK):
            self._attn_blk(h, blk)

    def _attn_blk(self, h, blk):
        k = self
        L = k.L
        NKB = L // 128
        QT = k.AR2[:, 0:L]
        KT = k.AR2[:, 4096:4096 + L]
        Vtok = k.AR2[:, 12288:12288 + L]
        bneg2 = k.kcol(-CSHIFT)

        HSLOT = [k.PSQ[2][:, 0:512], k.PSQ[2][:, 512:1024],
                 k.PSQ[3][:, 0:512], k.PSQ[3][:, 512:1024]]
        act_start = k.c["act"]
        pe_start = k.c["pe"]

        @k.block.tensor
        def _(t):
            for kb in range(NKB):
                if kb >= 4:
                    k.wait(t, "tensor", "act", act_start + (kb - 4) // 2 + 1)
                elif kb == 0:
                    if hasattr(k, "_p2_prev"):
                        k.wait(t, "tensor", "act", k._p2_prev)
                k.inc(t.matmul(HSLOT[kb % 4], KT[:, kb * 128:(kb + 1) * 128],
                               QT[:, blk * 512:(blk + 1) * 512],
                               start=True, stop=True), "pe")

        @k.block.scalar
        def _(a):
            for j in range(NKB // 2):
                k.wait(a, "scalar", "pe", pe_start + 2 * j + 2)
                if j == 0:
                    pv = getattr(k, "_av_pe_half", {}).get(blk % 2)
                    if pv is not None:
                        k.wait(a, "scalar", "pe", pv)
                buf = k.PSQ[2][:, :] if j % 2 == 0 else k.PSQ[3][:, :]
                pt = k.AR1[:, (blk % 2) * 16384 + j * 1024:(blk % 2) * 16384 + (j + 1) * 1024]
                k.inc(a.activation(out=pt, in_=buf, func=AF.Exp, bias=bneg2), "act")
        k._p2_prev = k.c["act"]

        av_half = k.PSQ[1][:, (blk % 2) * 512:(blk % 2) * 512 + 512]
        srow_ps = k.PSQ[0][0:1, 512:1024]

        @k.block.tensor
        def _(t):
            k.wait(t, "tensor", "act", k._p2_prev)
            if blk == 0:
                k.wait(t, "tensor", "dve", k._vt_last)
            prev = getattr(k, "_av_evac_half", {}).get(blk % 2)
            if prev is not None:
                k.wait(t, "tensor", "dve", prev)
            if hasattr(k, "_srow_evac"):
                k.wait(t, "tensor", "dve", k._srow_evac)
            ins = None
            for kb in range(NKB):
                pt = k.AR1[:, (blk % 2) * 16384 + kb * 512:(blk % 2) * 16384 + (kb + 1) * 512]
                t.matmul(av_half, Vtok[:, kb * 128:(kb + 1) * 128], pt,
                         start=(kb == 0), stop=(kb == NKB - 1))
                ins = t.matmul(srow_ps, k.c_onec[:, 0:1], pt,
                               start=(kb == 0), stop=(kb == NKB - 1))
            k.inc(ins, "pe")

        pe_at2 = k.c["pe"]
        if not hasattr(k, "_av_pe_half"):
            k._av_pe_half = {}
        k._av_pe_half[blk % 2] = pe_at2

        @k.block.vector
        def _(v):
            k.wait(v, "vector", "pe", pe_at2)
            k.wait(v, "vector", "dve", k.c["dve"])
            k._srow_evac = k.inc(v.tensor_copy(k.rowsb[0:1, 1536:2048], srow_ps), "dve")
            k.wait(v, "vector", "dve", k.c["dve"])
            k.inc(v.reciprocal(k.rowsb[0:1, 1024:1536],
                               k.rowsb[0:1, 1536:2048]), "dve")

        dve_cq = k.c["dve"]

        @k.block.gpsimd
        def _(g):
            k.wait(g, "gpsimd", "dve", dve_cq)
            k.dinc(g.dma_start(out=k.s_rowB.ap()[0:1, 1024:1536],
                               in_=k.rowsb[0:1, 1024:1536]), "qbn")
            k.wait(g, "gpsimd", "qbn", k.c["qbn"])
            k.dinc(g.dma_start(out=k.crep[:],
                               in_=bass.AP(k.s_rowB, 1024, [[0, 128], [1, 512]])), "qbn")

        qbn_at4 = k.c["qbn"]

        @k.block.vector
        def _(v):
            k.wait(v, "vector", "qbn", qbn_at4)
            stage = k.AR2[:, 16384 + (blk % 2) * 512:16384 + (blk % 2) * 512 + 512]
            sl = "qao%d" % (blk % 2)
            if sl in k.c:
                k.wait(v, "vector", sl, k.c[sl])
            at = k.inc(v.tensor_tensor(out=stage, in0=av_half, in1=k.crep[:],
                                       op=ALU.mult), "dve")
            if not hasattr(k, "_av_evac_half"):
                k._av_evac_half = {}
            k._av_evac_half[blk % 2] = at

        dve_at5 = k.c["dve"]

        @k.block.gpsimd
        def _(g):
            k.wait(g, "gpsimd", "dve", dve_at5)
            stage = k.AR2[:, 16384 + (blk % 2) * 512:16384 + (blk % 2) * 512 + 512]
            k.dinc(g.dma_start(out=k.s_aoT.ap()[h][:, blk * 512:(blk + 1) * 512],
                               in_=stage), "qao%d" % (blk % 2))

    def emit_dumps(self):
        k = self
        k.barrier()

        @k.block.sync
        def _(q):
            k.bwait(q, "sync")
            for name, out in k._dump_outs.items():
                src = getattr(k, name)
                if len(src.shape) == 3:
                    for i in range(src.shape[0]):
                        k.dinc(q.dma_start(out=out.ap()[i], in_=src.ap()[i]), "qdmp")
                else:
                    k.dinc(q.dma_start(out=out.ap(), in_=src.ap()), "qdmp")
            k.wait(q, "sync", "qdmp", k.c["qdmp"])

    def emit_finish(self):
        k = self
        k.barrier()

        @k.block.sync
        def _(q):
            k.bwait(q, "sync")
            q.nop()



def _prep_inputs(L0_attn_a_hidden_states, L0_attn_a_in_proj_w, L0_attn_a_conv_w,
                 L0_attn_a_conv_b, L0_attn_a_dt_bias, L0_attn_a_A_log, L0_attn_a_D,
                 L0_attn_a_norm_weight, L0_attn_a_out_proj_w, L0_attn_b_w_qkv,
                 L0_attn_b_b_qkv, L0_attn_b_w_out, L0_attn_b_b_out, L0_ffn_up_weight,
                 L0_ffn_down_weight, L0_ffn_up_scale, L0_ffn_down_scale,
                 L0_ffn_output_scale):
    f = np.float32
    winp = np.zeros((PROJ_P, D), f)
    winp[:8384] = np.asarray(L0_attn_a_in_proj_w, f)
    shared = {
        "winp": winp,
        "cwt": np.ascontiguousarray(
            np.asarray(L0_attn_a_conv_w, f).reshape(NCONV, 128, 4)
            .transpose(1, 0, 2).reshape(128, NCONV * 4)),
        "cbt": np.ascontiguousarray(np.asarray(L0_attn_a_conv_b, f).reshape(NCONV, 128).T),
        "dtb": np.asarray(L0_attn_a_dt_bias, f).reshape(64, 1),
        "alog": np.asarray(L0_attn_a_A_log, f).reshape(64, 1),
        "dvec": np.asarray(L0_attn_a_D, f).reshape(64, 1),
        "woutm": np.asarray(L0_attn_a_out_proj_w, f),
        "wqkv": np.asarray(L0_attn_b_w_qkv, f),
        "bqkvt": np.ascontiguousarray(np.asarray(L0_attn_b_b_qkv, f).reshape(48, 128).T),
        "wouta": np.asarray(L0_attn_b_w_out, f),
        "boutbc": np.ascontiguousarray(
            np.broadcast_to(np.asarray(L0_attn_b_b_out, f)[None, :], (128, D))),
        "wup": np.asarray(L0_ffn_up_weight, f),
        "wdn": np.asarray(L0_ffn_down_weight, f),
        "upsct": np.ascontiguousarray(np.asarray(L0_ffn_up_scale, f).reshape(64, 128).T),
        "dsct": np.ascontiguousarray(np.asarray(L0_ffn_down_scale, f).reshape(16, 128).T),
        "osct": np.ascontiguousarray(
            np.broadcast_to(np.asarray(L0_ffn_output_scale, f).reshape(1, 1), (128, 16))),
        "nwbc": np.ascontiguousarray(
            np.broadcast_to(np.asarray(L0_attn_a_norm_weight, f)[None, :], (128, INTER))),
        "idbf": np.eye(128).astype(ml_dtypes.bfloat16),
        "idf": np.eye(128, dtype=f),
        "ones1": np.ones((1, 128), ml_dtypes.bfloat16),
        "onec": np.ones((128, 1), ml_dtypes.bfloat16),
        "smlog": np.where(np.arange(128)[:, None] <= np.arange(128)[None, :],
                          0.0, MASK_NEG).astype(f),
    }
    x = np.asarray(L0_attn_a_hidden_states, f)
    return [dict(shared, x0=np.ascontiguousarray(x[b])) for b in range(x.shape[0])]


def kernel(**inputs):
    in_maps = _prep_inputs(**inputs)
    kb = KB(L=4096, phases=ALL_PHASES)
    nc = kb.build()
    res = run_bass_kernel_spmd(nc, in_maps, core_ids=list(range(len(in_maps))))
    return np.stack([r["y_out"] for r in res.results], axis=0)


# version 12
# speedup vs baseline: 1.1347x; 1.1347x over previous
import math
import numpy as np
import ml_dtypes
import contextlib
import concourse.bass as bass
import concourse.mybir as mybir
from concourse.bass_utils import run_bass_kernel_spmd

F32 = mybir.dt.float32
BF16 = mybir.dt.bfloat16
AF = mybir.ActivationFunctionType
ALU = mybir.AluOpType
AX = mybir.AxisListType

D = 2048
DC = 16
PROJ_P = 8448
PC = 66
NCONV = 33
INTER = 4096
FH = 8192
AH = 16
AE = 128
ASCALE = 1.0 / math.sqrt(AE)
LN_EPS = 1e-5
GN_EPS = 1e-6
MASK_NEG = -30000.0

ALL_PHASES = ("const", "prep", "mamba", "ffn1", "attn", "ffn2")

CSHIFT = 8.0
CONST_VALUES = (LN_EPS, GN_EPS, -0.5, -1.0, ASCALE, -ASCALE, 1.0, 0.0, -CSHIFT)

ENGS = ("pe", "dve", "act", "pool")


class KB:
    def __init__(self, L=4096, phases=ALL_PHASES, dumps=()):
        assert L % 512 == 0
        self.L = L
        self.TB = L // 128
        self.TR = L // 512
        self.phases = phases
        self.dumps = dumps
        self.nc = bass.Bass("TRN2", target_bir_lowering=False)
        self.c = {}
        self.sem = {}
        self._seen = {k: {} for k in ("tensor", "vector", "scalar", "gpsimd", "sync")}
        self._declare_dram()

    def _declare_dram(self):
        nc, L = self.nc, self.L
        di = lambda n, s, dt=F32: nc.dram_tensor(n, s, dt, kind="ExternalInput")
        ds = lambda n, s, dt: nc.dram_tensor(n, s, dt, kind="Internal")
        self.x0 = di("x0", [L, D])
        self.winp = di("winp", [PROJ_P, D])
        self.cwt = di("cwt", [128, NCONV * 4])
        self.cbt = di("cbt", [128, NCONV])
        self.dtb = di("dtb", [64, 1])
        self.alog = di("alog", [64, 1])
        self.dvec = di("dvec", [64, 1])
        self.woutm = di("woutm", [D, INTER])
        self.wqkv = di("wqkv", [D, 3 * D])
        self.bqkvt = di("bqkvt", [128, 48])
        self.wouta = di("wouta", [D, D])
        self.boutbc = di("boutbc", [128, D])
        self.wup = di("wup", [FH, D])
        self.wdn = di("wdn", [D, FH])
        self.upsct = di("upsct", [128, 64])
        self.dsct = di("dsct", [128, 16])
        self.osct = di("osct", [128, 16])
        self.nwbc = di("nwbc", [128, INTER])
        self.idbf = di("idbf", [128, 128], BF16)
        self.idf = di("idf", [128, 128])
        self.ones1 = di("ones1", [1, 128], BF16)
        self.onec = di("onec", [128, 1], BF16)
        self.smlog = di("smlog", [128, 128])

        self.y_out = nc.dram_tensor("y_out", [L, D], F32, kind="ExternalOutput")

        self.s_lnT = ds("s_lnT", [DC, 128, L], BF16)
        self.s_winT = ds("s_winT", [PC, 128, D], BF16)
        self.s_woutT = ds("s_woutT", [32, 128, D], BF16)
        self.s_wupT = ds("s_wupT", [64, 128, D], BF16)
        self.s_wdnT = ds("s_wdnT", [16, 128, FH], BF16)
        self.s_wqkv = ds("s_wqkv", [48, 128, D], BF16)
        self.s_wouta = ds("s_wouta", [16, 128, D], BF16)
        self.s_projT = ds("s_projT", [PC, 128, L], BF16)
        self.s_dtT = ds("s_dtT", [64, L], F32)
        self.s_dtT2 = ds("s_dtT2", [64, L], F32)
        self.s_adtT = ds("s_adtT", [64, L], F32)
        self.s_xbcT = ds("s_xbcT", [NCONV, 128, L], BF16)
        self.s_y = ds("s_y", [self.TB, 128, INTER], F32)
        self.s_qkvT = ds("s_qkvT", [48, 128, L], BF16)
        self.s_aoT = ds("s_aoT", [AH, 128, L], BF16)
        self.s_x1 = ds("s_x1", [L, D], F32)
        self.s_x2 = ds("s_x2", [L, D], F32)
        self.s_x3 = ds("s_x3", [L, D], F32)
        self.s_rowA = ds("s_rowA", [1, 8192], F32)
        self.s_rowB = ds("s_rowB", [1, 8192], F32)
        self._dump_outs = {}
        for name in self.dumps:
            src = getattr(self, name)
            self._dump_outs[name] = nc.dram_tensor(
                "dump_" + name, list(src.shape), src.dtype, kind="ExternalOutput")

    def q(self, name):
        if name not in self.sem:
            self.sem[name] = self._sempool.pop()
            self.c[name] = 0
        return name

    def inc(self, instr, name, k=1):
        self.q(name)
        instr.then_inc(self.sem[name], k)
        self.c[name] += k
        return self.c[name]

    def dinc(self, instr, name):
        return self.inc(instr, name, 16)

    def wait(self, eng, kind, name, val):
        self.q(name)
        eng.wait_ge(self.sem[name], val)
        if val > self._seen[kind].get(name, 0):
            self._seen[kind][name] = val

    def barrier(self):
        self.snap_ = dict(self.c)
        return self.snap_

    _OWN = {
        "tensor": ("pe",), "vector": ("dve",), "scalar": ("act",),
        "gpsimd": ("pool",), "sync": (),
    }

    def bwait(self, eng, kind):
        seen = self._seen[kind]
        for n, v in self.snap_.items():
            if v > 0 and n not in self._OWN[kind] and seen.get(n, 0) < v:
                eng.wait_ge(self.sem[n], v)
                seen[n] = v

    _SEDGE = {"vector": "dvs", "scalar": "acs", "gpsimd": "pos", "tensor": "pes"}

    def edge(self, eng, kind, instr):
        n = self._SEDGE[kind]
        self.inc(instr, n)
        self.wait(eng, kind, n, self.c[n])

    def kcol(self, val, rows=128):
        i = self._cvals[val]
        return self.CONSTV[0:rows, i:i + 1]

    def build(self):
        nc = self.nc
        with contextlib.ExitStack() as ctx:
            E = ctx.enter_context
            sb = lambda n, s, dt: E(nc.sbuf_tensor(n, s, dt))
            self.c_idbf = sb("c_idbf", [128, 128], BF16)
            self.c_idf = sb("c_idf", [128, 128], F32)
            self.c_ones = sb("c_ones", [1, 128], BF16)
            self.c_onec = sb("c_onec", [128, 1], BF16)
            self.c_smlog = sb("c_smlog", [128, 128], F32)
            self.p_cwt = sb("p_cwt", [128, NCONV * 4], F32)
            self.p_cbt = sb("p_cbt", [128, NCONV], F32)
            self.p_dtb = sb("p_dtb", [64, 1], F32)
            self.p_alog = sb("p_alog", [64, 1], F32)
            self.p_A = sb("p_A", [64, 1], F32)
            self.p_dvec = sb("p_dvec", [64, 1], F32)
            self.p_drep = sb("p_drep", [128, 64], F32)
            self.p_bqkv = sb("p_bqkv", [128, 48], F32)
            self.p_upsc = sb("p_upsc", [128, 64], F32)
            self.p_dsc = sb("p_dsc", [128, 16], F32)
            self.p_osc = sb("p_osc", [128, 16], F32)
            self.p_dscc = sb("p_dscc", [128, 16], F32)
            self.CONSTV = sb("CONSTV", [128, len(CONST_VALUES)], F32)
            self._cvals = {v: i for i, v in enumerate(CONST_VALUES)}
            self.AR1 = sb("AR1", [128, 32768], BF16)
            self.AR2 = sb("AR2", [128, 17408], BF16)
            self.AF1 = sb("AF1", [128, 8192], F32)
            self.AF2 = sb("AF2", [128, 2048], F32)
            self.XT = sb("XT", [128, 2048], F32)
            self.BOUT = sb("BOUT", [128, 2048], F32)
            self.ST_S = sb("ST_S", [64, INTER], F32)
            self.ST_BF = sb("ST_BF", [64, INTER], BF16)
            self.SM1 = sb("SM1", [128, 64], F32)
            self.YTMP = sb("YTMP", [128, 512], F32)
            self.YTMP2 = sb("YTMP2", [128, 512], F32)
            self.YTMP3 = sb("YTMP3", [128, 512], F32)
            self.cs0 = sb("cs0", [64, 128], F32)
            self.uu = sb("uu", [64, 128], F32)
            self.wb = sb("wb", [64, 128], F32)
            self.dtwb = sb("dtwb", [64, 128], F32)
            self.wb2 = sb("wb2", [64, 128], F32)
            self.z64 = sb("z64", [64, 128], F32)
            self.dt_r = sb("dt_r", [64, 512], F32)
            self.adt_r = sb("adt_r", [64, 512], F32)
            self.uerep = sb("uerep", [64, 64], F32)
            self.cstok = sb("cstok", [128, 64], F32)
            self.dttok = sb("dttok", [128, 64], F32)
            self.dwtok = sb("dwtok", [128, 64], F32)
            self.utok = sb("utok", [128, 64], F32)
            self.ccb = sb("ccb", [64, 128], BF16)
            self.btok = sb("btok", [128, 64], BF16)
            self.sraw = sb("sraw", [128, 128], F32)
            self.rowsb = sb("rowsb", [1, 2048], F32)
            self.crep = sb("crep", [128, 512], F32)
            self.r4 = sb("r4", [4, 128], F32)
            self.PSQ = [E(nc.psum_tensor("PSQ%d" % i, [128, 1024], F32))
                        for i in range(4)]
            self.PSE = self.PSQ[0][:, :]
            self.PSA = self.PSQ[1][:, 0:512]
            self.PSB = self.PSQ[1][:, 512:1024]
            self.PSC = self.PSQ[2][:, 0:512]
            self.PSF = self.PSQ[2][:, 512:1024]
            self.PSG = self.PSQ[3][:, 0:512]
            self.PSH = self.PSQ[3][:, 512:1024]
            self.PS4 = [self.PSA, self.PSB, self.PSC, self.PSF]
            self._sempool = [E(nc.semaphore("s%d" % i)) for i in range(90)][::-1]
            for n in ENGS:
                self.q(n)
            self.block = E(nc.Block())

            ph = self.phases
            if "const" in ph:
                self.emit_const()
            if "prep" in ph:
                self.emit_prep()
            if "mamba" in ph:
                self.emit_ln(self.x0)
                self.emit_minproj()
                self.emit_mconv()
                self.emit_mssd()
                self.emit_m5(self.x0, self.s_x1)
            if "ffn1" in ph:
                self.emit_ln(self.s_x1)
                self.emit_ffn(self.s_x1, self.s_x2)
            if "attn" in ph:
                self.emit_ln(self.s_x2)
                self.emit_attn(self.s_x2, self.s_x3)
            if "ffn2" in ph:
                self.emit_ln(self.s_x3)
                self.emit_ffn(self.s_x3, self.y_out)
            if self.dumps:
                self.emit_dumps()
            self.emit_finish()
        return self.nc

    def emit_const(self):
        k = self
        k.barrier()

        @k.block.sync
        def _(q):
            for dst, src in (
                (k.c_idbf, k.idbf), (k.c_idf, k.idf), (k.c_ones, k.ones1),
                (k.c_onec, k.onec),
                (k.c_smlog, k.smlog), (k.p_cwt, k.cwt), (k.p_cbt, k.cbt),
                (k.p_dtb, k.dtb), (k.p_alog, k.alog), (k.p_dvec, k.dvec),
                (k.p_bqkv, k.bqkvt), (k.p_upsc, k.upsct), (k.p_dsc, k.dsct),
                (k.p_osc, k.osct), (k.BOUT, k.boutbc),
            ):
                k.dinc(q.dma_start(out=dst[:], in_=src.ap()), "qc")

        nload = k.c["qc"]

        @k.block.scalar
        def _(a):
            k.wait(a, "scalar", "qc", nload)
            k.inc(a.activation(out=k.p_A[:], in_=k.p_alog[:], func=AF.Exp), "act")

        av = k.c["act"]

        @k.block.vector
        def _(v):
            for val, i in k._cvals.items():
                v.memset(k.CONSTV[:, i:i + 1], float(val))
            v.memset(k.z64[:], 0.0)
            k.wait(v, "vector", "act", av)
            k.inc(v.tensor_scalar_mul(k.p_A[:], k.p_A[:], -1.0), "dve")
            k.inc(v.tensor_tensor(out=k.p_dscc[:], in0=k.p_dsc[:],
                                  in1=k.p_osc[:], op=ALU.mult), "dve")

        @k.block.tensor
        def _(t):
            k.wait(t, "tensor", "qc", nload)
            k.inc(t.transpose(k.PSG[0:1, 0:64], k.p_dvec[:], k.c_idf[0:64, 0:64]), "pe")

        pv = k.c["pe"]

        @k.block.vector
        def _(v):
            k.wait(v, "vector", "pe", pv)
            k.inc(v.tensor_copy(k.rowsb[0:1, 0:64], k.PSG[0:1, 0:64]), "dve")

        dv2 = k.c["dve"]

        @k.block.gpsimd
        def _(g):
            k.wait(g, "gpsimd", "dve", dv2)
            k.dinc(g.dma_start(out=k.s_rowA.ap()[0:1, 0:64], in_=k.rowsb[0:1, 0:64]), "qrA")
            k.wait(g, "gpsimd", "qrA", k.c["qrA"])
            k.dinc(g.dma_start(out=k.p_drep[:],
                               in_=bass.AP(k.s_rowA, 0, [[0, 128], [1, 64]])), "qrA")

    def emit_prep(self):
        k = self
        self._prep_transpose(k.winp, PROJ_P, D, k.s_winT, sign=False)
        self._prep_transpose(k.wup, FH, D, k.s_wupT, sign=True)
        self._prep_transpose(k.wdn, D, FH, k.s_wdnT, sign=True)
        self._prep_transpose_col(k.woutm, D, INTER, k.s_woutT)
        self._prep_cast_qkv()
        self._prep_cast_rows(k.wouta, D, D, k.s_wouta)

    def _prep_transpose(self, src, R, C, dst, sign):
        k = self
        nj, ncch = R // 128, C // 128
        nslot = 2 if C <= 4096 else 1
        s = k.barrier()
        pb, ab = s.get("pe", 0), s.get("act", 0)
        qpw0 = [s.get("qpw0", 0), s.get("qpw1", 0)]
        qps0 = [s.get("qps0", 0), s.get("qps1", 0)]

        def lslot(j):
            return k.AF1[:, (j % nslot) * 4096:(j % nslot) * 4096 + C]

        def sslot(j):
            return k.AR2[:, (j % 2) * 8192:(j % 2) * 8192 + C]

        @k.block.sync
        def _(q):
            k.bwait(q, "sync")
            for j in range(nj):
                if j >= nslot:
                    k.wait(q, "sync", "pe", pb + (j - nslot + 1) * ncch)
                k.dinc(q.dma_start(out=lslot(j),
                                   in_=src.ap()[j * 128:(j + 1) * 128, :]),
                       "qpw%d" % (j % 2))

        @k.block.tensor
        def _(t):
            k.bwait(t, "tensor")
            for j in range(nj):
                k.wait(t, "tensor", "qpw%d" % (j % 2), qpw0[j % 2] + 16 * (j // 2 + 1))
                for c in range(ncch):
                    g = j * ncch + c
                    if g >= 4:
                        k.wait(t, "tensor", "act", ab + g - 3)
                    ps = k.PS4[g % 4]
                    k.inc(t.transpose(ps[:, 0:128], lslot(j)[:, c * 128:(c + 1) * 128],
                                      k.c_idf[:]), "pe")

        @k.block.scalar
        def _(a):
            k.bwait(a, "scalar")
            fn = AF.Sign if sign else AF.Copy
            for j in range(nj):
                for c in range(ncch):
                    g = j * ncch + c
                    k.wait(a, "scalar", "pe", pb + g + 1)
                    if c == 0 and j >= 2:
                        k.wait(a, "scalar", "qps%d" % (j % 2), qps0[j % 2] + 16 * (j // 2))
                    ps = k.PS4[g % 4]
                    k.inc(a.activation(out=sslot(j)[:, c * 128:(c + 1) * 128],
                                       in_=ps[:, 0:128], func=fn), "act")

        @k.block.gpsimd
        def _(g_):
            k.bwait(g_, "gpsimd")
            for j in range(nj):
                k.wait(g_, "gpsimd", "act", ab + (j + 1) * ncch)
                k.dinc(g_.dma_start(out=dst.ap()[j], in_=sslot(j)), "qps%d" % (j % 2))

    def _prep_transpose_col(self, src, R, C, dst):
        k = self
        ni, nrch = C // 128, R // 128
        s = k.barrier()
        pb, ab = s.get("pe", 0), s.get("act", 0)
        qpw0 = [s.get("qpw0", 0), s.get("qpw1", 0)]
        qps0 = [s.get("qps0", 0), s.get("qps1", 0)]

        def lslot(i):
            return k.AF1[:, (i % 2) * 4096:(i % 2) * 4096 + R]

        def sslot(i):
            return k.AR2[:, (i % 2) * 8192:(i % 2) * 8192 + R]

        @k.block.sync
        def _(q):
            k.bwait(q, "sync")
            for i in range(ni):
                if i >= 2:
                    k.wait(q, "sync", "pe", pb + (i - 1) * nrch)
                ap = src.ap()[:, i * 128:(i + 1) * 128].rearrange("(rj p) f -> p rj f", p=128)
                half = nrch // 2
                k.dinc(q.dma_start(
                    out=lslot(i)[:, 0:half * 128].rearrange("p (rj f) -> p rj f", f=128),
                    in_=ap[:, 0:half]), "qpw%d" % (i % 2))
                k.dinc(q.dma_start(
                    out=lslot(i)[:, half * 128:nrch * 128].rearrange("p (rj f) -> p rj f", f=128),
                    in_=ap[:, half:nrch]), "qpw%d" % (i % 2))

        @k.block.tensor
        def _(t):
            k.bwait(t, "tensor")
            for i in range(ni):
                k.wait(t, "tensor", "qpw%d" % (i % 2), qpw0[i % 2] + 32 * (i // 2 + 1))
                for rj in range(nrch):
                    g = i * nrch + rj
                    if g >= 4:
                        k.wait(t, "tensor", "act", ab + g - 3)
                    ps = k.PS4[g % 4]
                    k.inc(t.transpose(ps[:, 0:128], lslot(i)[:, rj * 128:(rj + 1) * 128],
                                      k.c_idf[:]), "pe")

        @k.block.scalar
        def _(a):
            k.bwait(a, "scalar")
            for i in range(ni):
                for rj in range(nrch):
                    g = i * nrch + rj
                    k.wait(a, "scalar", "pe", pb + g + 1)
                    if rj == 0 and i >= 2:
                        k.wait(a, "scalar", "qps%d" % (i % 2), qps0[i % 2] + 16 * (i // 2))
                    ps = k.PS4[g % 4]
                    k.inc(a.activation(out=sslot(i)[:, rj * 128:(rj + 1) * 128],
                                       in_=ps[:, 0:128], func=AF.Copy), "act")

        @k.block.gpsimd
        def _(g_):
            k.bwait(g_, "gpsimd")
            for i in range(ni):
                k.wait(g_, "gpsimd", "act", ab + (i + 1) * nrch)
                k.dinc(g_.dma_start(out=dst.ap()[i], in_=sslot(i)), "qps%d" % (i % 2))

    def _prep_cast_qkv(self):
        k = self
        s = k.barrier()
        db = s.get("dve", 0)
        qpw0 = s.get("qpw0", 0)
        qps0 = s.get("qps0", 0)

        @k.block.sync
        def _(q):
            k.bwait(q, "sync")
            for c in range(DC):
                if c >= 1:
                    k.wait(q, "sync", "dve", db + c)
                k.dinc(q.dma_start(out=k.AF1[:, 0:6144],
                                   in_=k.wqkv.ap()[c * 128:(c + 1) * 128, :]), "qpw0")

        @k.block.vector
        def _(v):
            k.bwait(v, "vector")
            for c in range(DC):
                k.wait(v, "vector", "qpw0", qpw0 + 16 * (c + 1))
                if c >= 2:
                    k.wait(v, "vector", "qps0", qps0 + 16 * 8 * (c - 1))
                k.inc(v.tensor_copy(k.AR2[:, (c % 2) * 8192:(c % 2) * 8192 + 6144],
                                    k.AF1[:, 0:6144]), "dve")

        @k.block.gpsimd
        def _(g):
            k.bwait(g, "gpsimd")
            for c in range(DC):
                k.wait(g, "gpsimd", "dve", db + c + 1)
                if c >= 1:
                    k.wait(g, "gpsimd", "qps0", k.c["qps0"])
                for j0 in range(0, 48, 6):
                    dstv = k.s_wqkv.ap()[j0:j0 + 6, :, c * 128:(c + 1) * 128].transpose([1, 0, 2])
                    srcv = k.AR2[:, (c % 2) * 8192 + j0 * 128:(c % 2) * 8192 + (j0 + 6) * 128] \
                        .rearrange("p (j f) -> p j f", f=128)
                    k.dinc(g.dma_start(out=dstv, in_=srcv), "qps0")

    def _prep_cast_rows(self, src, R, C, dst):
        k = self
        s = k.barrier()
        db = s.get("dve", 0)
        qpw0 = s.get("qpw0", 0)
        qps0 = s.get("qps0", 0)

        qpw1 = s.get("qpw1", 0)
        qpwb = [qpw0, qpw1]

        @k.block.sync
        def _(q):
            k.bwait(q, "sync")
            for i in range(R // 128):
                if i >= 2:
                    k.wait(q, "sync", "dve", db + i - 1)
                k.dinc(q.dma_start(out=k.AF1[:, (i % 2) * 4096:(i % 2) * 4096 + C],
                                   in_=src.ap()[i * 128:(i + 1) * 128, :]),
                       "qpw%d" % (i % 2))

        @k.block.vector
        def _(v):
            k.bwait(v, "vector")
            for i in range(R // 128):
                k.wait(v, "vector", "qpw%d" % (i % 2), qpwb[i % 2] + 16 * (i // 2 + 1))
                if i >= 2:
                    k.wait(v, "vector", "qps0", qps0 + 16 * (i - 1))
                k.inc(v.tensor_copy(k.AR2[:, (i % 2) * 8192:(i % 2) * 8192 + C],
                                    k.AF1[:, (i % 2) * 4096:(i % 2) * 4096 + C]), "dve")

        @k.block.gpsimd
        def _(g):
            k.bwait(g, "gpsimd")
            for i in range(R // 128):
                k.wait(g, "gpsimd", "dve", db + i + 1)
                if i >= 1:
                    k.wait(g, "gpsimd", "qps0", k.c["qps0"])
                k.dinc(g.dma_start(out=dst.ap()[i],
                                   in_=k.AR2[:, (i % 2) * 8192:(i % 2) * 8192 + C]), "qps0")

    def emit_ln(self, x_src):
        k = self
        TB = k.TB
        s = k.barrier()
        d0, a0, p0 = s.get("dve", 0), s.get("act", 0), s.get("pe", 0)
        f0 = s.get("qfl", 0)
        x0s = [s.get("qx0", 0), s.get("qx1", 0)]
        pse_bf = k.PSE[:].bitcast(BF16)

        @k.block.sync
        def _(q):
            k.bwait(q, "sync")
            for tb in range(TB):
                if tb >= 2:
                    k.wait(q, "sync", "dve", d0 + 5 * (tb - 2) + 3)
                xt = (k.XT if tb % 2 == 0 else k.AF2)[:, 0:2048]
                k.dinc(q.dma_start(out=xt, in_=x_src.ap()[tb * 128:(tb + 1) * 128, :]),
                       "qx%d" % (tb % 2))

        @k.block.scalar
        def _(a):
            k.bwait(a, "scalar")
            for tb in range(TB):
                xt = (k.XT if tb % 2 == 0 else k.AF2)[:, 0:2048]
                r8 = (tb % 8) * 8
                sq = k.SM1[:, r8 + 2:r8 + 3]
                var = k.SM1[:, r8 + 3:r8 + 4]
                rstd = k.SM1[:, r8 + 4:r8 + 5]
                k.wait(a, "scalar", "qx%d" % (tb % 2), x0s[tb % 2] + 16 * (tb // 2 + 1))
                k.inc(a.activation(out=k.AR1[:, 4096:6144], in_=xt, func=AF.Square,
                                   accum_out=sq), "act")
                k.wait(a, "scalar", "dve", d0 + 5 * tb + 2)
                ln_at = k.inc(a.activation(out=var, in_=var, func=AF.Ln, bias=k.kcol(LN_EPS)), "act")
                k.wait(a, "scalar", "act", ln_at)
                k.inc(a.activation(out=rstd, in_=var, func=AF.Exp, scale=k.kcol(-0.5)), "act")

        @k.block.vector
        def _(v):
            k.bwait(v, "vector")
            for tb in range(TB):
                xt = (k.XT if tb % 2 == 0 else k.AF2)[:, 0:2048]
                r8 = (tb % 8) * 8
                ssum = k.SM1[:, r8 + 0:r8 + 1]
                m = k.SM1[:, r8 + 1:r8 + 2]
                sq = k.SM1[:, r8 + 2:r8 + 3]
                var = k.SM1[:, r8 + 3:r8 + 4]
                rstd = k.SM1[:, r8 + 4:r8 + 5]
                msq = k.SM1[:, r8 + 5:r8 + 6]
                k.wait(v, "vector", "qx%d" % (tb % 2), x0s[tb % 2] + 16 * (tb // 2 + 1))
                k.edge(v, "vector", v.tensor_reduce(out=ssum, in_=xt, axis=AX.X, op=ALU.add))
                m_at = k.inc(v.tensor_scalar_mul(m, ssum, 1.0 / D), "dve")
                k.wait(v, "vector", "dve", m_at)
                k.edge(v, "vector", v.tensor_scalar_mul(msq, m, m))
                k.wait(v, "vector", "act", a0 + 3 * tb + 1)
                k.inc(v.tensor_scalar(out=var, in0=sq, scalar1=1.0 / D, scalar2=msq,
                                      op0=ALU.mult, op1=ALU.subtract), "dve")
                k.wait(v, "vector", "act", a0 + 3 * tb + 3)
                if tb >= 2:
                    k.wait(v, "vector", "pe", p0 + 16 * (tb - 1))
                lnb = k.AR1[:, (tb % 2) * 2048:(tb % 2) * 2048 + 2048]
                k.inc(v.tensor_scalar(out=lnb, in0=xt, scalar1=m, scalar2=rstd,
                                      op0=ALU.subtract, op1=ALU.mult), "dve")
                for h in range(2):
                    k.wait(v, "vector", "pe", p0 + 16 * tb + 8 * (h + 1))
                    if tb % 4 == 0 and tb > 0 and h == 0:
                        k.wait(v, "vector", "qfl", f0 + 256 * (tb // 4))
                    st = k.AR2[:, 0:8192].rearrange("p (c f) -> p c f", f=512)
                    src_v = pse_bf[:, h * 1024:(h + 1) * 1024].rearrange(
                        "p (c f) -> p c f", f=128)
                    k.inc(v.tensor_copy(
                        st[:, h * 8:(h + 1) * 8, (tb % 4) * 128:(tb % 4) * 128 + 128],
                        src_v), "dve")

        @k.block.tensor
        def _(t):
            k.bwait(t, "tensor")
            for tb in range(TB):
                k.wait(t, "tensor", "dve", d0 + 5 * tb + 3)
                lnb = k.AR1[:, (tb % 2) * 2048:(tb % 2) * 2048 + 2048]
                for h in range(2):
                    if tb > 0 or h > 0:
                        prev_evac = (d0 + 5 * tb + 4) if h == 1 else (d0 + 5 * (tb - 1) + 5)
                        k.wait(t, "tensor", "dve", prev_evac)
                    for cth in range(8):
                        cc = h * 8 + cth
                        k.inc(t.transpose(
                            pse_bf[:, h * 1024 + cth * 128:h * 1024 + (cth + 1) * 128],
                            lnb[:, cc * 128:(cc + 1) * 128], k.c_idbf[:]), "pe")

        @k.block.gpsimd
        def _(g):
            k.bwait(g, "gpsimd")
            for grp in range(TB // 4):
                k.wait(g, "gpsimd", "dve", d0 + 5 * (4 * grp + 4))
                for ch in range(DC):
                    st = k.AR2[:, ch * 512:(ch + 1) * 512]
                    k.dinc(g.dma_start(
                        out=k.s_lnT.ap()[ch][:, grp * 512:(grp + 1) * 512], in_=st), "qfl")

    def emit_minproj(self):
        k = self
        s = k.barrier()
        d0, p0 = s.get("dve", 0), s.get("pe", 0)
        l0 = s.get("qlnr", 0)
        w0s = [s.get("qws0", 0), s.get("qws1", 0)]
        st0s = [s.get("qst0", 0), s.get("qst1", 0), s.get("qst2", 0)]
        dt0 = s.get("qdt", 0)
        NJ = PC

        def devac(n):
            return n + n // NJ

        @k.block.sync
        def _(q):
            k.bwait(q, "sync")
            for tr in range(k.TR):
                if tr > 0:
                    k.wait(q, "sync", "pe", p0 + 16 * NJ * tr)
                for c in range(DC):
                    k.dinc(q.dma_start(
                        out=k.AR2[:, c * 512:(c + 1) * 512],
                        in_=k.s_lnT.ap()[c][:, tr * 512:(tr + 1) * 512]), "qlnr")

        @k.block.scalar
        def _(a):
            k.bwait(a, "scalar")
            for tr in range(k.TR):
                for j in range(NJ):
                    i = tr * NJ + j
                    if i >= 2:
                        k.wait(a, "scalar", "pe", p0 + 16 * (i - 1))
                    ws = k.AR2[:, 8192 + (i % 2) * 2048:8192 + (i % 2) * 2048 + 2048]
                    k.dinc(a.dma_start(out=ws, in_=k.s_winT.ap()[j]), "qws%d" % (i % 2))

        @k.block.tensor
        def _(t):
            k.bwait(t, "tensor")
            for tr in range(k.TR):
                k.wait(t, "tensor", "qlnr", l0 + 16 * DC * (tr + 1))
                for j in range(NJ):
                    i = tr * NJ + j
                    k.wait(t, "tensor", "qws%d" % (i % 2), w0s[i % 2] + 16 * (i // 2 + 1))
                    if i >= 3:
                        k.wait(t, "tensor", "dve", d0 + devac(i - 2))
                    ws = k.AR2[:, 8192 + (i % 2) * 2048:8192 + (i % 2) * 2048 + 2048]
                    ps = k.PS4[i % 3]
                    for c in range(DC):
                        k.inc(t.matmul(ps[:], ws[:, c * 128:(c + 1) * 128],
                                       k.AR2[:, c * 512:(c + 1) * 512],
                                       start=(c == 0), stop=(c == DC - 1)), "pe")

        @k.block.vector
        def _(v):
            k.bwait(v, "vector")
            for tr in range(k.TR):
                for j in range(NJ):
                    i = tr * NJ + j
                    k.wait(v, "vector", "pe", p0 + 16 * (i + 1))
                    if i >= 3:
                        sl = i % 3
                        k.wait(v, "vector", "qst%d" % sl,
                               st0s[sl] + 16 * ((i - 3) // 3 + 1))
                    if j == NJ - 1 and tr >= 2:
                        k.wait(v, "vector", "qdt", dt0 + 16 * tr)
                    ps = k.PS4[i % 3]
                    stg = k.AR1[:, (i % 3) * 512:(i % 3) * 512 + 512]
                    k.inc(v.tensor_copy(stg, ps[:]), "dve")
                    if j == NJ - 1:
                        k.inc(v.tensor_copy(
                            k.AF2[0:64, (tr % 2) * 512:(tr % 2) * 512 + 512],
                            ps[0:64, :]), "dve")

        @k.block.gpsimd
        def _(g):
            k.bwait(g, "gpsimd")
            for tr in range(k.TR):
                for j in range(NJ):
                    i = tr * NJ + j
                    k.wait(g, "gpsimd", "dve", d0 + devac(i + 1))
                    stg = k.AR1[:, (i % 3) * 512:(i % 3) * 512 + 512]
                    k.dinc(g.dma_start(
                        out=k.s_projT.ap()[j][:, tr * 512:(tr + 1) * 512], in_=stg),
                        "qst%d" % (i % 3))
                    if j == NJ - 1:
                        k.dinc(g.dma_start(
                            out=k.s_dtT.ap()[:, tr * 512:(tr + 1) * 512],
                            in_=k.AF2[0:64, (tr % 2) * 512:(tr % 2) * 512 + 512]), "qdt")

    def emit_mconv(self):
        k = self
        L = k.L
        s = k.barrier()
        d0, a0, o0 = s.get("dve", 0), s.get("act", 0), s.get("pool", 0)
        cv0 = [s.get("qcv0", 0), s.get("qcv1", 0)]
        cs0_ = [s.get("qcs0", 0), s.get("qcs1", 0)]

        @k.block.sync
        def _(q):
            k.bwait(q, "sync")
            for ch in range(NCONV):
                if ch >= 2:
                    k.wait(q, "sync", "dve", d0 + ch - 1)
                k.dinc(q.dma_start(out=k.AR2[:, (ch % 2) * 4096:(ch % 2) * 4096 + L],
                                   in_=k.s_projT.ap()[32 + ch]), "qcv%d" % (ch % 2))

        def conv_ops(eng, kind, ch):
            half = ch % 2
            xin = k.AR2[:, (ch % 2) * 4096:(ch % 2) * 4096 + L]
            acc = k.AF1[:, half * 4096:half * 4096 + L]
            w = lambda kk: k.p_cwt[:, ch * 4 + kk:ch * 4 + kk + 1]
            k.wait(eng, kind, "qcv%d" % (ch % 2), cv0[ch % 2] + 16 * (ch // 2 + 1))
            if ch > 1:
                k.wait(eng, kind, "act", a0 + ch - 1)
            k.edge(eng, kind, eng.tensor_scalar(
                out=acc, in0=xin, scalar1=w(3), scalar2=k.p_cbt[:, ch:ch + 1],
                op0=ALU.mult, op1=ALU.add))
            k.edge(eng, kind, eng.scalar_tensor_tensor(
                out=acc[:, 1:L], in0=xin[:, 0:L - 1], scalar=w(2),
                in1=acc[:, 1:L], op0=ALU.mult, op1=ALU.add))
            k.edge(eng, kind, eng.scalar_tensor_tensor(
                out=acc[:, 2:L], in0=xin[:, 0:L - 2], scalar=w(1),
                in1=acc[:, 2:L], op0=ALU.mult, op1=ALU.add))
            k.inc(eng.scalar_tensor_tensor(
                out=acc[:, 3:L], in0=xin[:, 0:L - 3], scalar=w(0),
                in1=acc[:, 3:L], op0=ALU.mult, op1=ALU.add), "dve")

        @k.block.vector
        def _(v):
            k.bwait(v, "vector")
            for ch in range(NCONV):
                conv_ops(v, "vector", ch)

        @k.block.scalar
        def _(a):
            k.bwait(a, "scalar")
            for ch in range(NCONV):
                half = ch % 2
                acc = k.AF1[:, half * 4096:half * 4096 + L]
                stg = k.AR1[:, (ch % 2) * 4096:(ch % 2) * 4096 + L]
                k.wait(a, "scalar", "dve", d0 + ch + 1)
                if ch >= 2:
                    k.wait(a, "scalar", "qcs%d" % (ch % 2), cs0_[ch % 2] + 16 * (ch // 2))
                k.inc(a.activation(out=stg, in_=acc, func=AF.Silu), "act")

        @k.block.gpsimd
        def _(g):
            k.bwait(g, "gpsimd")
            for ch in range(NCONV):
                stg = k.AR1[:, (ch % 2) * 4096:(ch % 2) * 4096 + L]
                k.wait(g, "gpsimd", "act", a0 + ch + 1)
                k.dinc(g.dma_start(out=k.s_xbcT.ap()[ch], in_=stg), "qcs%d" % (ch % 2))

        s = k.barrier()

        @k.block.sync
        def _(q):
            k.bwait(q, "sync")

        for c0 in range(0, L, 2048):
            s = k.barrier()
            cw = min(2048, L - c0)

            @k.block.sync
            def _(q):
                k.bwait(q, "sync")
                k.dinc(q.dma_start(out=k.AF2[0:64, 0:cw],
                                   in_=k.s_dtT.ap()[:, c0:c0 + cw]), "qdl")

            qv = k.c["qdl"]

            @k.block.vector
            def _(v):
                k.bwait(v, "vector")
                k.wait(v, "vector", "qdl", qv)
                k.inc(v.tensor_scalar_add(k.AF2[0:64, 0:cw], k.AF2[0:64, 0:cw],
                                          k.p_dtb[:]), "dve")

            dv = k.c["dve"]

            @k.block.scalar
            def _(a):
                k.bwait(a, "scalar")
                k.wait(a, "scalar", "dve", dv)
                t = k.XT[0:64, 0:cw]
                k.edge(a, "scalar", a.activation(out=t, in_=k.AF2[0:64, 0:cw], func=AF.Abs))
                k.edge(a, "scalar", a.activation(out=k.AF1[0:64, 0:cw], in_=t, func=AF.Exp,
                                                 scale=k.kcol(-1.0, rows=64)))
                k.inc(a.activation(out=t, in_=k.AF1[0:64, 0:cw], func=AF.Ln,
                                   bias=k.kcol(1.0, rows=64)), "act")

            av = k.c["act"]

            @k.block.vector
            def _(v):
                k.bwait(v, "vector")
                k.wait(v, "vector", "act", av)
                k.edge(v, "vector", v.scalar_tensor_tensor(
                    out=k.AF2[0:64, 0:cw], in0=k.AF2[0:64, 0:cw], scalar=0.0,
                    in1=k.XT[0:64, 0:cw], op0=ALU.max, op1=ALU.add))
                k.inc(v.tensor_scalar_mul(k.XT[0:64, 0:cw], k.AF2[0:64, 0:cw],
                                          k.p_A[:]), "dve")

            dv2 = k.c["dve"]

            @k.block.gpsimd
            def _(g):
                k.bwait(g, "gpsimd")
                k.wait(g, "gpsimd", "dve", dv2)
                k.dinc(g.dma_start(out=k.s_dtT2.ap()[:, c0:c0 + cw],
                                   in_=k.AF2[0:64, 0:cw]), "qdl2")
                k.dinc(g.dma_start(out=k.s_adtT.ap()[:, c0:c0 + cw],
                                   in_=k.XT[0:64, 0:cw]), "qdl2")

    def emit_mssd(self):
        k = self
        L = k.L
        pse_bf = k.PSE[:].bitcast(BF16)

        k.barrier()

        @k.block.gpsimd
        def _(g):
            k.bwait(g, "gpsimd")
            k.inc(g.memset(k.ST_S[:], 0.0), "pool")

        @k.block.vector
        def _(v):
            k.bwait(v, "vector")
            k.inc(v.memset(k.ST_BF[:], 0.0), "dve")

        k._sp = None
        for rg in range(L // 512):
            if rg == 0:
                k.barrier()
            sp = k._sp

            @k.block.sync
            def _(q):
                if rg == 0:
                    k.bwait(q, "sync")
                else:
                    k.wait(q, "sync", "pe", sp["pe"])
                    k.wait(q, "sync", "pool", sp["pool"])
                    k.wait(q, "sync", "qrg", sp["qrg"])
                for ch in range(32):
                    k.dinc(q.dma_start(
                        out=k.AR2[:, ch * 512:(ch + 1) * 512],
                        in_=k.s_xbcT.ap()[ch][:, rg * 512:(rg + 1) * 512]), "qsx")
                k.dinc(q.dma_start(out=k.AR2[:, 16384:16896],
                                   in_=k.s_xbcT.ap()[32][:, rg * 512:(rg + 1) * 512]), "qsx")
                k.dinc(q.dma_start(out=k.dt_r[:], in_=k.s_dtT2.ap()[:, rg * 512:(rg + 1) * 512]), "qsx")
                k.dinc(q.dma_start(out=k.adt_r[:], in_=k.s_adtT.ap()[:, rg * 512:(rg + 1) * 512]), "qsx")

            for cc in range(4):
                self._ssd_chunk(rg, cc)

    def _ssd_chunk(self, rg, cc):
        k = self
        gc = rg * 4 + cc
        pse_bf = k.PSE[:].bitcast(BF16)
        s = dict(k.c)
        sp = k._sp
        d0, a0, p0, o0 = s.get("dve", 0), s.get("act", 0), s.get("pe", 0), s.get("pool", 0)
        qsx0 = s.get("qsx", 0)
        xs_tok = k.AR1[:, 0:4096]
        dtx = k.AR1[:, 4096:8192]
        dtxwb = k.AR1[:, 8192:12288]
        Lm = k.AR1[:, 12288:20480]
        Sm = k.AR1[:, 20480:28672]
        dxs = k.AR1[:, 28672:32768]
        v3 = lambda buf: buf.rearrange("p (h t) -> p h t", h=64)
        v64 = lambda buf: buf.rearrange("p (h e) -> p h e", h=64)
        btile = k.AR2[:, 16384 + cc * 128:16384 + (cc + 1) * 128]

        @k.block.vector
        def _(v):
            if sp is None:
                k.bwait(v, "vector")
            else:
                k.wait(v, "vector", "act", sp["a2"])
                k.wait(v, "vector", "pe", sp["p5"])
                k.wait(v, "vector", "qrg", sp["qrg"])
            k.wait(v, "vector", "qsx", k.c["qsx"])
            at = k.inc(v.tensor_tensor_scan(
                out=k.cs0[:], data0=k.adt_r[:, cc * 128:(cc + 1) * 128],
                data1=k.z64[:], initial=0.0, op0=ALU.add, op1=ALU.add), "dve")
            k.wait(v, "vector", "dve", at)
            k.inc(v.tensor_scalar(out=k.wb[:], in0=k.cs0[:],
                                  scalar1=k.cs0[:, 127:128], scalar2=None,
                                  op0=ALU.subtract), "dve")

        @k.block.scalar
        def _(a):
            k.bwait(a, "scalar")
            k.wait(a, "scalar", "dve", d0 + 1)
            k.inc(a.activation(out=k.uu[:], in_=k.cs0[:], func=AF.Exp), "act")
            k.wait(a, "scalar", "dve", d0 + 2)
            k.inc(a.activation(out=k.wb2[:], in_=k.wb[:], func=AF.Exp,
                               scale=k.kcol(-1.0, rows=64)), "act")

        @k.block.vector
        def _(v):
            k.wait(v, "vector", "act", a0 + 2)
            k.inc(v.tensor_tensor(out=k.dtwb[:], in0=k.dt_r[:, cc * 128:(cc + 1) * 128],
                                  in1=k.wb2[:], op=ALU.mult), "dve")

        @k.block.tensor
        def _(t):
            if sp is None:
                k.bwait(t, "tensor")
            else:
                k.wait(t, "tensor", "dve", sp["v3"])
                k.wait(t, "tensor", "dve", sp["v6"])
            srcs = None
            k.wait(t, "tensor", "dve", d0 + 1)
            k.inc(t.transpose(k.PSG[:, 0:64], k.cs0[:], k.c_idf[0:64, 0:64]), "pe")
            k.inc(t.transpose(k.PSH[:, 0:64], k.dt_r[:, cc * 128:(cc + 1) * 128],
                              k.c_idf[0:64, 0:64]), "pe")
            k.wait(t, "tensor", "dve", d0 + 4)
            k.wait(t, "tensor", "dve", d0 + 3)
            k.inc(t.transpose(k.PSG[:, 64:128], k.dtwb[:], k.c_idf[0:64, 0:64]), "pe")
            k.wait(t, "tensor", "dve", d0 + 5)
            k.wait(t, "tensor", "act", a0 + 1)
            k.inc(t.transpose(k.PSH[:, 64:128], k.uu[:], k.c_idf[0:64, 0:64]), "pe")
            k.wait(t, "tensor", "dve", d0 + 6)
            k.inc(t.transpose(k.PSG[0:1, 128:192], k.uu[:, 127:128],
                              k.c_idf[0:64, 0:64]), "pe")
            for b in range(4):
                if b >= 2:
                    k.wait(t, "tensor", "dve", d0 + 8 + (b - 1))
                for j in range(8):
                    chn = b * 8 + j
                    half = (b % 2) * 1024
                    src = k.AR2[:, chn * 512 + cc * 128:chn * 512 + cc * 128 + 128]
                    ins = t.transpose(pse_bf[:, half + j * 128:half + (j + 1) * 128],
                                      src, k.c_idbf[:])
                    if j == 7:
                        k.inc(ins, "pe")
            k.wait(t, "tensor", "dve", d0 + 11)
            k.inc(t.transpose(pse_bf[:, 0:64], btile[0:64, :], k.c_idbf[0:64, 0:64]), "pe")

        @k.block.vector
        def _(v):
            k.wait(v, "vector", "pe", p0 + 1)
            k.inc(v.tensor_copy(k.cstok[:], k.PSG[:, 0:64]), "dve")
            k.wait(v, "vector", "pe", p0 + 2)
            k.inc(v.tensor_copy(k.dttok[:], k.PSH[:, 0:64]), "dve")
            k.wait(v, "vector", "pe", p0 + 3)
            k.inc(v.tensor_copy(k.dwtok[:], k.PSG[:, 64:128]), "dve")
            k.wait(v, "vector", "pe", p0 + 4)
            k.inc(v.tensor_copy(k.utok[:], k.PSH[:, 64:128]), "dve")
            k.wait(v, "vector", "pe", p0 + 5)
            k.inc(v.tensor_copy(k.rowsb[0:1, 0:64], k.PSG[0:1, 128:192]), "dve")
            for b in range(4):
                k.wait(v, "vector", "pe", p0 + 6 + b)
                half = (b % 2) * 1024
                k.inc(v.tensor_copy(xs_tok[:, b * 1024:(b + 1) * 1024],
                                    pse_bf[:, half:half + 1024]), "dve")
            k.wait(v, "vector", "pe", p0 + 10)
            k.inc(v.tensor_copy(k.btok[:], pse_bf[:, 0:64]), "dve")
            k.wait(v, "vector", "dve", d0 + 13)
            k.inc(v.tensor_tensor(out=v64(dtx), in0=v64(xs_tok),
                                  in1=k.dttok[:].unsqueeze(2).broadcast_to([128, 64, 64]),
                                  op=ALU.mult), "dve")
            k.inc(v.tensor_tensor(out=v64(dxs), in0=v64(xs_tok),
                                  in1=k.p_drep[:].unsqueeze(2).broadcast_to([128, 64, 64]),
                                  op=ALU.mult), "dve")
            k.inc(v.tensor_tensor(out=v64(dtxwb), in0=v64(xs_tok),
                                  in1=k.dwtok[:].unsqueeze(2).broadcast_to([128, 64, 64]),
                                  op=ALU.mult), "dve")

        @k.block.gpsimd
        def _(g):
            if sp is None:
                k.bwait(g, "gpsimd")
            else:
                k.wait(g, "gpsimd", "pe", sp["pe"])
            k.wait(g, "gpsimd", "qsx", k.c["qsx"])
            k.inc(g.tensor_copy(k.ccb[:], btile[64:128, :]), "pool")
            k.wait(g, "gpsimd", "dve", d0 + 1)
            k.dinc(g.dma_start(
                out=k.s_rowA.ap().rearrange("o (h t) -> (o h) t", h=64),
                in_=k.cs0[:]), "qrg")
            k.wait(g, "gpsimd", "dve", d0 + 8)
            k.dinc(g.dma_start(out=k.s_rowB.ap()[0:1, 0:64],
                               in_=k.rowsb[0:1, 0:64]), "qrg")
            k.wait(g, "gpsimd", "qrg", k.c["qrg"])
            if sp is not None:
                k.wait(g, "gpsimd", "act", sp["a3"])
                k.wait(g, "gpsimd", "dve", sp["v6"])
            k.dinc(g.dma_start(out=k.AF1[:],
                               in_=bass.AP(k.s_rowA, 0, [[0, 128], [1, 8192]])), "qrb")
            k.dinc(g.dma_start(out=k.uerep[:],
                               in_=bass.AP(k.s_rowB, 0, [[0, 64], [1, 64]])), "qrb")

        qrb_v = k.c["qrb"]

        @k.block.vector
        def _(v):
            k.wait(v, "vector", "qrb", qrb_v)
            k.inc(v.tensor_tensor(
                out=v3(k.AF1[:]), in0=v3(k.AF1[:]),
                in1=k.cstok[:].unsqueeze(2).broadcast_to([128, 64, 128]),
                op=ALU.subtract), "dve")

        @k.block.gpsimd
        def _(g):
            k.wait(g, "gpsimd", "dve", d0 + 17)
            k.inc(g.tensor_tensor(
                out=v3(k.AF1[:]), in0=v3(k.AF1[:]),
                in1=k.c_smlog[:].unsqueeze(1).broadcast_to([128, 64, 128]),
                op=ALU.add), "pool")

        @k.block.scalar
        def _(a):
            k.wait(a, "scalar", "pool", o0 + 2)
            if sp is not None:
                k.wait(a, "scalar", "pool", sp["o3"])
            k.inc(a.activation(out=Lm, in_=k.AF1[:], func=AF.Exp), "act")

        @k.block.tensor
        def _(t):
            k.wait(t, "tensor", "pool", o0 + 1)
            if sp is not None:
                k.wait(t, "tensor", "dve", sp["v5"])
            k.inc(t.matmul(k.PSH[:, 128:256], btile[0:64, :], k.ccb[:],
                           start=True, stop=True), "pe")

        @k.block.vector
        def _(v):
            k.wait(v, "vector", "pe", p0 + 11)
            if sp is not None:
                k.wait(v, "vector", "pool", sp["o3"])
            k.inc(v.tensor_copy(k.sraw[:], k.PSH[:, 128:256]), "dve")

        @k.block.gpsimd
        def _(g):
            k.wait(g, "gpsimd", "act", a0 + 3)
            k.wait(g, "gpsimd", "dve", d0 + 18)
            k.inc(g.tensor_tensor(
                out=v3(Sm), in0=v3(Lm),
                in1=k.sraw[:].unsqueeze(1).broadcast_to([128, 64, 128]),
                op=ALU.mult), "pool")

        @k.block.tensor
        def _(t):
            k.wait(t, "tensor", "dve", d0 + 18)
            k.wait(t, "tensor", "pool", o0 + 3)
            if sp is not None:
                k.wait(t, "tensor", "dve", sp["v6"])
            for g in range(8):
                pd = [k.PSA, k.PSB][g % 2]
                po = [k.PSC, k.PSF][g % 2]
                pst = k.PSE[0:64, (g % 2) * 512:(g % 2) * 512 + 512]
                if g >= 2:
                    k.wait(t, "tensor", "dve", d0 + 18 + 3 * (g - 1))
                k.inc(t.matmul(po[:], k.ccb[:], k.ST_BF[:, g * 512:(g + 1) * 512],
                               start=True, stop=True), "pe")
                ins = t.matmul(pd[:], k.c_idbf[:], dxs[:, g * 512:(g + 1) * 512],
                               start=True, stop=False)
                for hh in range(8):
                    h = g * 8 + hh
                    ins = t.matmul(pd[:, hh * 64:(hh + 1) * 64],
                                   Sm[:, h * 128:(h + 1) * 128],
                                   dtx[:, h * 64:(h + 1) * 64],
                                   start=False, stop=(hh == 7))
                k.inc(ins, "pe")
                k.inc(t.matmul(pst, k.btok[:], dtxwb[:, g * 512:(g + 1) * 512],
                               start=True, stop=True), "pe")

        @k.block.vector
        def _(v):
            for g in range(8):
                pd = [k.PSA, k.PSB][g % 2]
                po = [k.PSC, k.PSF][g % 2]
                pst = k.PSE[0:64, (g % 2) * 512:(g % 2) * 512 + 512]
                stage = [k.YTMP2, k.YTMP3][g % 2]
                k.wait(v, "vector", "pe", p0 + 11 + 3 * g + 2)
                k.wait(v, "vector", "qys%d" % (g % 2),
                       s.get("qys%d" % (g % 2), 0) + 16 * (g // 2))
                k.wait(v, "vector", "dve", k.c["dve"])
                at = k.inc(v.tensor_tensor(
                    out=k.YTMP[:].rearrange("p (h e) -> p h e", h=8),
                    in0=po[:].rearrange("p (h e) -> p h e", h=8),
                    in1=k.utok[:, rg * 0 + 8 * g:8 * g + 8].unsqueeze(2).broadcast_to([128, 8, 64]),
                    op=ALU.mult), "dve")
                k.wait(v, "vector", "dve", at)
                k.inc(v.tensor_tensor(out=stage[:], in0=k.YTMP[:], in1=pd[:],
                                      op=ALU.add), "dve")
                k.wait(v, "vector", "pe", p0 + 11 + 3 * g + 3)
                k.inc(v.tensor_tensor(out=k.ST_S[:, g * 512:(g + 1) * 512],
                                      in0=pst, in1=k.ST_S[:, g * 512:(g + 1) * 512],
                                      op=ALU.add), "dve")
            k.wait(v, "vector", "qrb", qrb_v)
            k.wait(v, "vector", "dve", k.c["dve"])
            at = k.inc(v.tensor_tensor(
                out=k.ST_S[:].rearrange("p (h e) -> p h e", h=64),
                in0=k.ST_S[:].rearrange("p (h e) -> p h e", h=64),
                in1=k.uerep[:].unsqueeze(2).broadcast_to([64, 64, 64]),
                op=ALU.mult), "dve")
            k.wait(v, "vector", "dve", at)
            k.inc(v.tensor_copy(k.ST_BF[:], k.ST_S[:]), "dve")

        @k.block.gpsimd
        def _(g_):
            for g in range(8):
                stage = [k.YTMP2, k.YTMP3][g % 2]
                k.wait(g_, "gpsimd", "dve", d0 + 18 + 3 * g + 2)
                k.dinc(g_.dma_start(
                    out=k.s_y.ap()[gc][:, g * 512:(g + 1) * 512], in_=stage[:]),
                    "qys%d" % (g % 2))

        k._sp = {
            "a2": a0 + 2, "a3": a0 + 3, "p5": p0 + 5, "pe": k.c["pe"],
            "v3": d0 + 16, "v5": d0 + 18, "v6": k.c["dve"], "o3": o0 + 3,
            "pool": k.c["pool"], "qrg": k.c["qrg"],
        }

    def emit_m5(self, x_src, x_dst):
        k = self
        L = k.L
        pse_bf = k.PSE[:].bitcast(BF16)
        PS8 = [k.PSA, k.PSB, k.PSC, k.PSF, k.PSG, k.PSH]
        YGT = lambda: k.AR1[:, 0:16384]
        nwbf = k.AR1[:, 28672:32768]

        k.barrier()

        @k.block.sync
        def _(q):
            k.bwait(q, "sync")
            k.dinc(q.dma_start(out=k.AF1[:, 0:4096], in_=k.nwbc.ap()), "qnw")

        qv = k.c["qnw"]

        @k.block.vector
        def _(v):
            k.bwait(v, "vector")
            k.wait(v, "vector", "qnw", qv)
            k.inc(v.tensor_copy(nwbf, k.AF1[:, 0:4096]), "dve")

        for rg in range(L // 512):
            k.barrier()

            @k.block.sync
            def _(q):
                k.bwait(q, "sync")
                for j in range(32):
                    k.dinc(q.dma_start(
                        out=k.AR2[:, j * 512:(j + 1) * 512],
                        in_=k.s_projT.ap()[j][:, rg * 512:(rg + 1) * 512]), "qsz")

            for tb in range(4):
                s = k.barrier()
                d0, a0, p0 = s.get("dve", 0), s.get("act", 0), s.get("pe", 0)
                qz0 = s.get("qsz", 0)
                z_tok = k.AR1[:, 16384:20480]
                zs = k.AR1[:, 20480:24576]
                scrap = k.AR1[:, 24576:28672]
                y = k.AF1[:, 0:4096]
                yg1 = k.AF1[:, 4096:8192]

                @k.block.sync
                def _(q):
                    k.bwait(q, "sync")
                    k.dinc(q.dma_start(out=y, in_=k.s_y.ap()[rg * 4 + tb]), "qyl")

                qyv = k.c["qyl"]

                @k.block.tensor
                def _(t):
                    k.bwait(t, "tensor")
                    for b in range(2):
                        if b == 1:
                            k.wait(t, "tensor", "dve", d0 + 1)
                        for j in range(16):
                            jj = b * 16 + j
                            src = k.AR2[:, jj * 512 + tb * 128:jj * 512 + tb * 128 + 128]
                            ins = t.transpose(pse_bf[:, j * 128:(j + 1) * 128],
                                              src, k.c_idbf[:])
                            if j == 15:
                                k.inc(ins, "pe")

                @k.block.vector
                def _(v):
                    k.bwait(v, "vector")
                    for b in range(2):
                        k.wait(v, "vector", "pe", p0 + 1 + b)
                        k.inc(v.tensor_copy(z_tok[:, b * 2048:(b + 1) * 2048],
                                            pse_bf[:, 0:2048]), "dve")

                @k.block.scalar
                def _(a):
                    k.bwait(a, "scalar")
                    k.wait(a, "scalar", "dve", d0 + 2)
                    k.inc(a.activation(out=zs, in_=z_tok, func=AF.Silu), "act")
                    k.wait(a, "scalar", "qyl", qyv)
                    k.inc(a.activation(out=scrap, in_=y, func=AF.Square,
                                       accum_out=k.SM1[:, 0:1]), "act")

                @k.block.vector
                def _(v):
                    k.wait(v, "vector", "act", a0 + 2)
                    k.inc(v.tensor_scalar_mul(k.SM1[:, 1:2], k.SM1[:, 0:1],
                                              1.0 / INTER), "dve")

                @k.block.scalar
                def _(a):
                    k.wait(a, "scalar", "dve", d0 + 3)
                    at = k.inc(a.activation(out=k.SM1[:, 2:3], in_=k.SM1[:, 1:2],
                                            func=AF.Ln, bias=k.kcol(GN_EPS)), "act")
                    k.wait(a, "scalar", "act", at)
                    k.inc(a.activation(out=k.SM1[:, 3:4], in_=k.SM1[:, 2:3],
                                       func=AF.Exp, scale=k.kcol(-0.5)), "act")

                @k.block.vector
                def _(v):
                    k.wait(v, "vector", "act", a0 + 4)
                    at = k.inc(v.scalar_tensor_tensor(
                        out=yg1, in0=y, scalar=k.SM1[:, 3:4], in1=nwbf,
                        op0=ALU.mult, op1=ALU.mult), "dve")
                    k.wait(v, "vector", "dve", at)
                    k.inc(v.tensor_tensor(out=scrap, in0=yg1, in1=zs,
                                          op=ALU.mult), "dve")

                @k.block.tensor
                def _(t):
                    k.wait(t, "tensor", "dve", d0 + 5)
                    for b in range(2):
                        if b == 1:
                            k.wait(t, "tensor", "dve", d0 + 6)
                        for j in range(16):
                            jj = b * 16 + j
                            ins = t.transpose(pse_bf[:, j * 128:(j + 1) * 128],
                                              scrap[:, jj * 128:(jj + 1) * 128],
                                              k.c_idbf[:])
                            if j == 15:
                                k.inc(ins, "pe")

                @k.block.vector
                def _(v):
                    for b in range(2):
                        k.wait(v, "vector", "pe", p0 + 3 + b)
                        st = YGT().rearrange("p (i f) -> p i f", f=512)
                        k.inc(v.tensor_copy(
                            st[:, b * 16:(b + 1) * 16, tb * 128:tb * 128 + 128],
                            pse_bf[:, 0:2048].rearrange("p (i f) -> p i f", f=128)),
                            "dve")

            s = k.barrier()
            d0, p0 = s.get("dve", 0), s.get("pe", 0)
            w0s = [s.get("qwo0", 0), s.get("qwo1", 0)]
            x0s = [s.get("qxr0", 0), s.get("qxr1", 0)]
            m0s = [s.get("qms0", 0), s.get("qms1", 0)]
            PSL = PS8 + [k.PSE[:, 0:512], k.PSE[:, 512:1024]]

            @k.block.scalar
            def _(a):
                k.bwait(a, "scalar")
                for half in range(2):
                    for i in range(32):
                        ii = half * 32 + i
                        if ii >= 2:
                            k.wait(a, "scalar", "pe", p0 + (ii - 1))
                        ws = k.AR2[:, (ii % 2) * 2048:(ii % 2) * 2048 + 2048]
                        k.dinc(a.dma_start(out=ws, in_=k.s_woutT.ap()[i]),
                               "qwo%d" % (ii % 2))

            @k.block.tensor
            def _(t):
                k.bwait(t, "tensor")
                for half in range(2):
                    for i in range(32):
                        ii = half * 32 + i
                        k.wait(t, "tensor", "qwo%d" % (ii % 2),
                               w0s[ii % 2] + 16 * (ii // 2 + 1))
                        if i == 0 and half == 1:
                            k.wait(t, "tensor", "dve", d0 + 8)
                        ws = k.AR2[:, (ii % 2) * 2048:(ii % 2) * 2048 + 2048]
                        for t2 in range(2):
                            for dj in range(4):
                                ins = t.matmul(
                                    PSL[t2 * 4 + dj][:],
                                    YGT()[:, i * 512 + (half * 2 + t2) * 128:
                                          i * 512 + (half * 2 + t2) * 128 + 128],
                                    ws[:, dj * 512:(dj + 1) * 512],
                                    start=(i == 0), stop=(i == 31))
                        k.inc(ins, "pe")

            @k.block.sync
            def _(q):
                k.bwait(q, "sync")
                for half in range(2):
                    for t2 in range(2):
                        tb_abs = half * 2 + t2
                        xt = (k.XT if tb_abs % 2 == 0 else k.AF2)[:, 0:2048]
                        if tb_abs >= 2:
                            k.wait(q, "sync", "dve", d0 + 4 * (tb_abs - 1))
                        row = (rg * 4 + tb_abs) * 128
                        k.dinc(q.dma_start(out=xt, in_=x_src.ap()[row:row + 128, :]),
                               "qxr%d" % (tb_abs % 2))

            @k.block.vector
            def _(v):
                k.bwait(v, "vector")
                for half in range(2):
                    k.wait(v, "vector", "pe", p0 + 32 * (half + 1))
                    for t2 in range(2):
                        tb_abs = half * 2 + t2
                        xt = (k.XT if tb_abs % 2 == 0 else k.AF2)[:, 0:2048]
                        stage = k.AF1[:, (tb_abs % 2) * 2048:(tb_abs % 2) * 2048 + 2048]
                        k.wait(v, "vector", "qxr%d" % (tb_abs % 2),
                               x0s[tb_abs % 2] + 16 * (tb_abs // 2 + 1))
                        if tb_abs >= 2:
                            k.wait(v, "vector", "qms%d" % (tb_abs % 2),
                                   m0s[tb_abs % 2] + 16 * (tb_abs // 2))
                        k.wait(v, "vector", "dve", k.c["dve"])
                        for dj in range(4):
                            ins = v.tensor_tensor(
                                out=stage[:, dj * 512:(dj + 1) * 512],
                                in0=PSL[t2 * 4 + dj][:], in1=xt[:, dj * 512:(dj + 1) * 512],
                                op=ALU.add)
                        k.inc(ins, "dve", 4)

            @k.block.gpsimd
            def _(g):
                k.bwait(g, "gpsimd")
                for half in range(2):
                    for t2 in range(2):
                        tb_abs = half * 2 + t2
                        stage = k.AF1[:, (tb_abs % 2) * 2048:(tb_abs % 2) * 2048 + 2048]
                        k.wait(g, "gpsimd", "dve", d0 + 4 * (tb_abs + 1))
                        row = (rg * 4 + tb_abs) * 128
                        k.dinc(g.dma_start(out=x_dst.ap()[row:row + 128, :],
                                           in_=stage), "qms%d" % (tb_abs % 2))

    def emit_ffn(self, x_src, x_dst):
        k = self
        L = k.L
        for rg in range(L // 512):
            s = k.barrier()
            d0, p0, a0 = s.get("dve", 0), s.get("pe", 0), s.get("act", 0)
            w0s = [s.get("qws0", 0), s.get("qws1", 0)]
            u0s = [s.get("qwu0", 0), s.get("qwu1", 0)]
            x0s = [s.get("qxr0", 0), s.get("qxr1", 0)]
            m0s = [s.get("qms0", 0), s.get("qms1", 0)]
            l0 = s.get("qlnr", 0)

            @k.block.sync
            def _(q):
                k.bwait(q, "sync")
                for c in range(DC):
                    k.dinc(q.dma_start(
                        out=k.AR2[:, c * 512:(c + 1) * 512],
                        in_=k.s_lnT.ap()[c][:, rg * 512:(rg + 1) * 512]), "qlnr")
                for tb in range(4):
                    xt = (k.XT if tb % 2 == 0 else k.AF2)[:, 0:2048]
                    if tb >= 2:
                        k.wait(q, "sync", "dve", d0 + 160 + (tb - 1))
                    row = (rg * 4 + tb) * 128
                    k.dinc(q.dma_start(out=xt, in_=x_src.ap()[row:row + 128, :]),
                           "qxr%d" % (tb % 2))

            @k.block.scalar
            def _(a):
                k.bwait(a, "scalar")
                for hj in range(64):
                    if hj >= 2:
                        k.wait(a, "scalar", "pe", p0 + hj - 1)
                    ws = k.AR2[:, 8192 + (hj % 2) * 2048:8192 + (hj % 2) * 2048 + 2048]
                    k.dinc(a.dma_start(out=ws, in_=k.s_wupT.ap()[hj]), "qws%d" % (hj % 2))

            @k.block.tensor
            def _(t):
                k.bwait(t, "tensor")
                k.wait(t, "tensor", "qlnr", l0 + 16 * DC)
                for hj in range(64):
                    k.wait(t, "tensor", "qws%d" % (hj % 2), w0s[hj % 2] + 16 * (hj // 2 + 1))
                    if hj >= 3:
                        k.wait(t, "tensor", "dve", d0 + hj - 2)
                    ws = k.AR2[:, 8192 + (hj % 2) * 2048:8192 + (hj % 2) * 2048 + 2048]
                    ps = [k.PSA, k.PSB, k.PSC][hj % 3]
                    ins = None
                    for c in range(DC):
                        ins = t.matmul(ps[:], ws[:, c * 128:(c + 1) * 128],
                                       k.AR2[:, c * 512:(c + 1) * 512],
                                       start=(c == 0), stop=(c == DC - 1))
                    k.inc(ins, "pe")

            @k.block.vector
            def _(v):
                k.bwait(v, "vector")
                for hj in range(64):
                    k.wait(v, "vector", "pe", p0 + hj + 1)
                    ps = [k.PSA, k.PSB, k.PSC][hj % 3]
                    k.inc(v.tensor_scalar(
                        out=k.AR1[:, hj * 512:(hj + 1) * 512], in0=ps[:],
                        scalar1=k.p_upsc[:, hj:hj + 1], scalar2=0.0,
                        op0=ALU.mult, op1=ALU.max), "dve")

            dn_p0 = p0 + 64

            @k.block.gpsimd
            def _(g):
                k.bwait(g, "gpsimd")
                for j in range(16):
                    for qq in range(4):
                        iq = j * 4 + qq
                        if iq >= 2:
                            pj, pq = (iq - 2) // 4, (iq - 2) % 4
                            k.wait(g, "gpsimd", "pe", dn_p0 + 8 * pj + pq + 1)
                        ws = k.AR2[:, 12288 + (iq % 2) * 2048:12288 + (iq % 2) * 2048 + 2048]
                        k.dinc(g.dma_start(
                            out=ws, in_=k.s_wdnT.ap()[j][:, qq * 2048:(qq + 1) * 2048]),
                            "qwu%d" % (iq % 2))

            @k.block.tensor
            def _(t):
                k.wait(t, "tensor", "dve", d0 + 64)
                for j in range(16):
                    ps = [k.PSF, k.PSG][j % 2]
                    if j >= 2:
                        k.wait(t, "tensor", "dve", d0 + 64 + 6 * (j - 1))
                    for qq in range(4):
                        iq = j * 4 + qq
                        k.wait(t, "tensor", "qwu%d" % (iq % 2),
                               u0s[iq % 2] + 16 * (iq // 2 + 1))
                        ws = k.AR2[:, 12288 + (iq % 2) * 2048:12288 + (iq % 2) * 2048 + 2048]
                        ins = None
                        for hh in range(16):
                            hchunk = qq * 16 + hh
                            ins = t.matmul(ps[:], ws[:, hh * 128:(hh + 1) * 128],
                                           k.AR1[:, hchunk * 512:(hchunk + 1) * 512],
                                           start=(iq % 4 == 0 and hh == 0),
                                           stop=(qq == 3 and hh == 15))
                        k.inc(ins, "pe")
                    k.wait(t, "tensor", "dve", d0 + 64 + 6 * j + 1)
                    if j >= 1:
                        k.wait(t, "tensor", "dve", d0 + 64 + 6 * j)
                    ytd = [k.YTMP, k.YTMP2][j % 2]
                    for t4 in range(4):
                        k.inc(t.transpose(k.PSH[:, t4 * 128:(t4 + 1) * 128],
                                          ytd[:, t4 * 128:(t4 + 1) * 128],
                                          k.c_idf[:]), "pe")

            @k.block.vector
            def _(v):
                for j in range(16):
                    ps = [k.PSF, k.PSG][j % 2]
                    ytd = [k.YTMP, k.YTMP2][j % 2]
                    k.wait(v, "vector", "pe", dn_p0 + 8 * j + 4)
                    k.wait(v, "vector", "dve", k.c["dve"])
                    k.inc(v.tensor_scalar(
                        out=ytd[:], in0=ps[:], scalar1=k.p_dscc[:, j:j + 1],
                        scalar2=None, op0=ALU.mult), "dve")
                    k.wait(v, "vector", "pe", dn_p0 + 8 * j + 8)
                    st = k.AF1[:].rearrange("p (tb f) -> p tb f", f=2048)
                    k.inc(v.tensor_copy(
                        st[:, :, j * 128:(j + 1) * 128],
                        k.PSH[:].rearrange("p (tb f) -> p tb f", f=128)), "dve", 5)

            @k.block.vector
            def _(v):
                for tb in range(4):
                    xt = (k.XT if tb % 2 == 0 else k.AF2)[:, 0:2048]
                    stage = k.AF1[:, tb * 2048:(tb + 1) * 2048]
                    k.wait(v, "vector", "qxr%d" % (tb % 2), x0s[tb % 2] + 16 * (tb // 2 + 1))
                    k.wait(v, "vector", "dve", d0 + 64 + 96)
                    if tb >= 2:
                        k.wait(v, "vector", "qms%d" % (tb % 2), m0s[tb % 2] + 16 * (tb // 2))
                    k.inc(v.tensor_tensor(out=stage, in0=stage, in1=xt, op=ALU.add),
                          "dve")

            @k.block.gpsimd
            def _(g):
                for tb in range(4):
                    stage = k.AF1[:, tb * 2048:(tb + 1) * 2048]
                    k.wait(g, "gpsimd", "dve", d0 + 64 + 96 + tb + 1)
                    row = (rg * 4 + tb) * 128
                    k.dinc(g.dma_start(out=x_dst.ap()[row:row + 128, :], in_=stage),
                           "qms%d" % (tb % 2))

    def emit_attn(self, x_src, x_dst):
        k = self
        L = k.L
        pse_bf = k.PSE[:].bitcast(BF16)

        for rg in range(L // 512):
            s = k.barrier()
            d0, p0 = s.get("dve", 0), s.get("pe", 0)
            w0s = [s.get("qws0", 0), s.get("qws1", 0)]
            st0s = [s.get("qst0", 0), s.get("qst1", 0), s.get("qst2", 0)]
            l0 = s.get("qlnr", 0)

            @k.block.sync
            def _(q):
                k.bwait(q, "sync")
                for c in range(DC):
                    k.dinc(q.dma_start(
                        out=k.AR2[:, c * 512:(c + 1) * 512],
                        in_=k.s_lnT.ap()[c][:, rg * 512:(rg + 1) * 512]), "qlnr")

            @k.block.scalar
            def _(a):
                k.bwait(a, "scalar")
                for j in range(48):
                    if j >= 2:
                        k.wait(a, "scalar", "pe", p0 + j - 1)
                    ws = k.AR2[:, 8192 + (j % 2) * 2048:8192 + (j % 2) * 2048 + 2048]
                    k.dinc(a.dma_start(out=ws, in_=k.s_wqkv.ap()[j]), "qws%d" % (j % 2))

            @k.block.tensor
            def _(t):
                k.bwait(t, "tensor")
                k.wait(t, "tensor", "qlnr", l0 + 16 * DC)
                for j in range(48):
                    k.wait(t, "tensor", "qws%d" % (j % 2), w0s[j % 2] + 16 * (j // 2 + 1))
                    if j >= 3:
                        k.wait(t, "tensor", "dve", d0 + j - 2)
                    ws = k.AR2[:, 8192 + (j % 2) * 2048:8192 + (j % 2) * 2048 + 2048]
                    ps = [k.PSA, k.PSB, k.PSC][j % 3]
                    ins = None
                    for c in range(DC):
                        ins = t.matmul(ps[:], ws[:, c * 128:(c + 1) * 128],
                                       k.AR2[:, c * 512:(c + 1) * 512],
                                       start=(c == 0), stop=(c == DC - 1))
                    k.inc(ins, "pe")

            @k.block.vector
            def _(v):
                k.bwait(v, "vector")
                for j in range(48):
                    k.wait(v, "vector", "pe", p0 + j + 1)
                    if j >= 3:
                        sl = j % 3
                        k.wait(v, "vector", "qst%d" % sl,
                               st0s[sl] + 16 * ((j - 3) // 3 + 1))
                    ps = [k.PSA, k.PSB, k.PSC][j % 3]
                    stg = k.AR1[:, (j % 3) * 512:(j % 3) * 512 + 512]
                    if j < 16:
                        k.inc(v.tensor_scalar(out=stg, in0=ps[:],
                                              scalar1=k.p_bqkv[:, j:j + 1],
                                              scalar2=ASCALE,
                                              op0=ALU.add, op1=ALU.mult), "dve")
                    else:
                        k.inc(v.tensor_scalar_add(stg, ps[:], k.p_bqkv[:, j:j + 1]), "dve")

            @k.block.gpsimd
            def _(g):
                k.bwait(g, "gpsimd")
                for j in range(48):
                    k.wait(g, "gpsimd", "dve", d0 + j + 1)
                    stg = k.AR1[:, (j % 3) * 512:(j % 3) * 512 + 512]
                    k.dinc(g.dma_start(
                        out=k.s_qkvT.ap()[j][:, rg * 512:(rg + 1) * 512], in_=stg),
                        "qst%d" % (j % 3))

        for h in range(AH):
            self._attn_head(h)

        k.barrier()

        @k.block.sync
        def _(q):
            k.bwait(q, "sync")
            for i in range(16):
                k.dinc(q.dma_start(out=k.AR1[:, i * 2048:(i + 1) * 2048],
                                   in_=k.s_wouta.ap()[i]), "qoaw")

        qoaw_v = k.c["qoaw"]
        for rg in range(L // 512):
            s = k.barrier()
            d0, p0, o0 = s.get("dve", 0), s.get("pe", 0), s.get("pool", 0)
            a0s = s.get("qaor", 0)
            x0s = [s.get("qxr0", 0), s.get("qxr1", 0)]
            m0s = [s.get("qms0", 0), s.get("qms1", 0)]

            @k.block.sync
            def _(q):
                k.bwait(q, "sync")
                k.wait(q, "sync", "qoaw", qoaw_v)
                for i in range(16):
                    k.dinc(q.dma_start(
                        out=k.AR2[:, i * 512:(i + 1) * 512],
                        in_=k.s_aoT.ap()[i][:, rg * 512:(rg + 1) * 512]), "qaor")
                for tb in range(4):
                    xt = (k.XT if tb % 2 == 0 else k.AF2)[:, 0:2048]
                    if tb >= 2:
                        k.wait(q, "sync", "pool", o0 + (tb - 1))
                    row = (rg * 4 + tb) * 128
                    k.dinc(q.dma_start(out=xt, in_=x_src.ap()[row:row + 128, :]),
                           "qxr%d" % (tb % 2))

            @k.block.gpsimd
            def _(g):
                k.bwait(g, "gpsimd")
                for tb in range(4):
                    xt = (k.XT if tb % 2 == 0 else k.AF2)[:, 0:2048]
                    xb = k.AF1[:, 4096 + (tb % 2) * 2048:4096 + (tb % 2) * 2048 + 2048]
                    k.wait(g, "gpsimd", "qxr%d" % (tb % 2), x0s[tb % 2] + 16 * (tb // 2 + 1))
                    if tb >= 2:
                        k.wait(g, "gpsimd", "dve", d0 + 4 * (tb - 1))
                    k.inc(g.tensor_tensor(out=xb, in0=xt, in1=k.BOUT[:], op=ALU.add),
                          "pool")

            @k.block.tensor
            def _(t):
                k.bwait(t, "tensor")
                k.wait(t, "tensor", "qaor", a0s + 16 * 16)
                for tb in range(4):
                    if tb >= 2:
                        k.wait(t, "tensor", "dve", d0 + 4 * (tb - 1))
                    ins = None
                    for i in range(16):
                        for dj in range(4):
                            pslist = ([k.PSA[:], k.PSB[:], k.PSC[:], k.PSF[:]]
                                      if tb % 2 == 0 else
                                      [k.PSG[:], k.PSH[:], k.PSE[:, 0:512], k.PSE[:, 512:1024]])
                            ins = t.matmul(
                                pslist[dj],
                                k.AR2[:, i * 512 + tb * 128:i * 512 + tb * 128 + 128],
                                k.AR1[:, i * 2048 + dj * 512:i * 2048 + (dj + 1) * 512],
                                start=(i == 0), stop=(i == 15))
                    k.inc(ins, "pe")

            @k.block.vector
            def _(v):
                k.bwait(v, "vector")
                for tb in range(4):
                    xb = k.AF1[:, 4096 + (tb % 2) * 2048:4096 + (tb % 2) * 2048 + 2048]
                    stage = k.AF1[:, (tb % 2) * 2048:(tb % 2) * 2048 + 2048]
                    k.wait(v, "vector", "pe", p0 + tb + 1)
                    k.wait(v, "vector", "pool", o0 + tb + 1)
                    if tb >= 2:
                        k.wait(v, "vector", "qms%d" % (tb % 2), m0s[tb % 2] + 16 * (tb // 2))
                    PSLt = ([k.PSA[:], k.PSB[:], k.PSC[:], k.PSF[:]] if tb % 2 == 0
                            else [k.PSG[:], k.PSH[:], k.PSE[:, 0:512], k.PSE[:, 512:1024]])
                    ins = None
                    for dj in range(4):
                        ins = v.tensor_tensor(out=stage[:, dj * 512:(dj + 1) * 512],
                                              in0=PSLt[dj],
                                              in1=xb[:, dj * 512:(dj + 1) * 512],
                                              op=ALU.add)
                    k.inc(ins, "dve", 4)

            @k.block.gpsimd
            def _(g):
                for tb in range(4):
                    stage = k.AF1[:, (tb % 2) * 2048:(tb % 2) * 2048 + 2048]
                    k.wait(g, "gpsimd", "dve", d0 + 4 * (tb + 1))
                    row = (rg * 4 + tb) * 128
                    k.dinc(g.dma_start(out=x_dst.ap()[row:row + 128, :], in_=stage),
                           "qms%d" % (tb % 2))

    def _attn_head(self, h):
        k = self
        L = k.L
        NQB = L // 128
        NKB = L // 128
        NBLK = L // 512
        pse_bf = k.PSE[:].bitcast(BF16)
        QT = k.AR2[:, 0:4096][:, 0:L]
        KT = k.AR2[:, 4096:4096 + L]
        VT = k.AR2[:, 8192:8192 + L]
        Vtok = k.AR2[:, 12288:12288 + L]
        scrap = k.AR2[:, 16384:17408]
        pe_head0 = k.c.get("pe", 0)
        qbn0 = k.c.get("qbn", 0)
        qao_now = {s_: k.c.get(s_, 0) for s_ in ("qao0", "qao1")}
        if h == 0:
            s = k.barrier()

        @k.block.sync
        def _(q):
            if h == 0:
                k.bwait(q, "sync")
            else:
                k.wait(q, "sync", "pe", pe_head0)
            k.dinc(q.dma_start(out=QT, in_=k.s_qkvT.ap()[h][:, 0:L]), "qhd")

        @k.block.scalar
        def _(a):
            if h == 0:
                k.bwait(a, "scalar")
            else:
                k.wait(a, "scalar", "pe", pe_head0)
            k.dinc(a.dma_start(out=KT, in_=k.s_qkvT.ap()[16 + h][:, 0:L]), "qhk")

        @k.block.gpsimd
        def _(g):
            if h == 0:
                k.bwait(g, "gpsimd")
            else:
                k.wait(g, "gpsimd", "pe", k._pe_vt_done)
            k.dinc(g.dma_start(out=VT, in_=k.s_qkvT.ap()[32 + h][:, 0:L]), "qhv")

        qhd_v = k.c["qhd"]
        qhk_v = k.c["qhk"]
        qhv_v = k.c["qhv"]

        nbat = (NKB + 7) // 8
        k._vt_evac = {}
        for b in range(nbat):
            n_t = min(8, NKB - b * 8)

            @k.block.tensor
            def _(t):
                if b == 0:
                    if h == 0:
                        k.bwait(t, "tensor")
                    k.wait(t, "tensor", "qhd", qhd_v)
                    k.wait(t, "tensor", "qhk", qhk_v)
                    k.wait(t, "tensor", "qhv", qhv_v)
                    if hasattr(k, "_srow_evac"):
                        k.wait(t, "tensor", "dve", k._srow_evac)
                if b >= 2:
                    k.wait(t, "tensor", "dve", k._vt_evac[b - 2])
                half = (b % 2) * 1024
                ins = None
                for j in range(n_t):
                    kb = b * 8 + j
                    ins = t.transpose(pse_bf[:, half + j * 128:half + (j + 1) * 128],
                                      VT[:, kb * 128:(kb + 1) * 128], k.c_idbf[:])
                k.inc(ins, "pe")

            pe_at = k.c["pe"]

            @k.block.vector
            def _(v):
                if b == 0 and h == 0:
                    k.bwait(v, "vector")
                k.wait(v, "vector", "pe", pe_at)
                half = (b % 2) * 1024
                at = k.inc(v.tensor_copy(Vtok[:, b * 1024:b * 1024 + n_t * 128],
                                         pse_bf[:, half:half + n_t * 128]), "dve")
                k._vt_evac[b] = at

        k._vt_last = k._vt_evac[max(k._vt_evac)]
        k._pe_vt_done = k.c["pe"]

        for blk in range(NBLK):
            self._attn_blk(h, blk)

    def _attn_blk(self, h, blk):
        k = self
        L = k.L
        NKB = L // 128
        QT = k.AR2[:, 0:L]
        KT = k.AR2[:, 4096:4096 + L]
        Vtok = k.AR2[:, 12288:12288 + L]
        bneg2 = k.kcol(-CSHIFT)

        HSLOT = [k.PSQ[2][:, 0:512], k.PSQ[2][:, 512:1024],
                 k.PSQ[3][:, 0:512], k.PSQ[3][:, 512:1024]]
        act_start = k.c["act"]
        pe_start = k.c["pe"]

        @k.block.tensor
        def _(t):
            for kb in range(NKB):
                if kb >= 4:
                    k.wait(t, "tensor", "act", act_start + (kb - 4) // 2 + 1)
                elif kb == 0:
                    if hasattr(k, "_p2_prev"):
                        k.wait(t, "tensor", "act", k._p2_prev)
                k.inc(t.matmul(HSLOT[kb % 4], KT[:, kb * 128:(kb + 1) * 128],
                               QT[:, blk * 512:(blk + 1) * 512],
                               start=True, stop=True), "pe")

        @k.block.scalar
        def _(a):
            for j in range(NKB // 2):
                k.wait(a, "scalar", "pe", pe_start + 2 * j + 2)
                if j == 0:
                    pv = getattr(k, "_av_pe_half", {}).get(blk % 2)
                    if pv is not None:
                        k.wait(a, "scalar", "pe", pv)
                buf = k.PSQ[2][:, :] if j % 2 == 0 else k.PSQ[3][:, :]
                pt = k.AR1[:, (blk % 2) * 16384 + j * 1024:(blk % 2) * 16384 + (j + 1) * 1024]
                k.inc(a.activation(out=pt, in_=buf, func=AF.Exp, bias=bneg2), "act")
        k._p2_prev = k.c["act"]

        av_half = k.PSQ[1][:, (blk % 2) * 512:(blk % 2) * 512 + 512]
        srow_ps = k.PSQ[0][0:1, 512:1024]

        @k.block.tensor
        def _(t):
            k.wait(t, "tensor", "act", k._p2_prev)
            if blk == 0:
                k.wait(t, "tensor", "dve", k._vt_last)
            prev = getattr(k, "_av_evac_half", {}).get(blk % 2)
            if prev is not None:
                k.wait(t, "tensor", "dve", prev)
            if hasattr(k, "_srow_evac"):
                k.wait(t, "tensor", "dve", k._srow_evac)
            ins = None
            for kb in range(NKB):
                pt = k.AR1[:, (blk % 2) * 16384 + kb * 512:(blk % 2) * 16384 + (kb + 1) * 512]
                t.matmul(av_half, Vtok[:, kb * 128:(kb + 1) * 128], pt,
                         start=(kb == 0), stop=(kb == NKB - 1))
                ins = t.matmul(srow_ps, k.c_onec[:, 0:1], pt,
                               start=(kb == 0), stop=(kb == NKB - 1))
            k.inc(ins, "pe")

        pe_at2 = k.c["pe"]
        if not hasattr(k, "_av_pe_half"):
            k._av_pe_half = {}
        k._av_pe_half[blk % 2] = pe_at2

        @k.block.vector
        def _(v):
            k.wait(v, "vector", "pe", pe_at2)
            k.wait(v, "vector", "dve", k.c["dve"])
            k._srow_evac = k.inc(v.tensor_copy(k.rowsb[0:1, 1536:2048], srow_ps), "dve")
            k.wait(v, "vector", "dve", k.c["dve"])
            k.inc(v.reciprocal(k.rowsb[0:1, 1024:1536],
                               k.rowsb[0:1, 1536:2048]), "dve")

        dve_cq = k.c["dve"]

        @k.block.gpsimd
        def _(g):
            k.wait(g, "gpsimd", "dve", dve_cq)
            k.dinc(g.dma_start(out=k.s_rowB.ap()[0:1, 1024:1536],
                               in_=k.rowsb[0:1, 1024:1536]), "qbn")
            k.wait(g, "gpsimd", "qbn", k.c["qbn"])
            k.dinc(g.dma_start(out=k.crep[:],
                               in_=bass.AP(k.s_rowB, 1024, [[0, 128], [1, 512]])), "qbn")

        qbn_at4 = k.c["qbn"]

        @k.block.vector
        def _(v):
            k.wait(v, "vector", "qbn", qbn_at4)
            stage = k.AR2[:, 16384 + (blk % 2) * 512:16384 + (blk % 2) * 512 + 512]
            sl = "qao%d" % (blk % 2)
            if sl in k.c:
                k.wait(v, "vector", sl, k.c[sl])
            at = k.inc(v.tensor_tensor(out=stage, in0=av_half, in1=k.crep[:],
                                       op=ALU.mult), "dve")
            if not hasattr(k, "_av_evac_half"):
                k._av_evac_half = {}
            k._av_evac_half[blk % 2] = at

        dve_at5 = k.c["dve"]

        @k.block.gpsimd
        def _(g):
            k.wait(g, "gpsimd", "dve", dve_at5)
            stage = k.AR2[:, 16384 + (blk % 2) * 512:16384 + (blk % 2) * 512 + 512]
            k.dinc(g.dma_start(out=k.s_aoT.ap()[h][:, blk * 512:(blk + 1) * 512],
                               in_=stage), "qao%d" % (blk % 2))

    def emit_dumps(self):
        k = self
        k.barrier()

        @k.block.sync
        def _(q):
            k.bwait(q, "sync")
            for name, out in k._dump_outs.items():
                src = getattr(k, name)
                if len(src.shape) == 3:
                    for i in range(src.shape[0]):
                        k.dinc(q.dma_start(out=out.ap()[i], in_=src.ap()[i]), "qdmp")
                else:
                    k.dinc(q.dma_start(out=out.ap(), in_=src.ap()), "qdmp")
            k.wait(q, "sync", "qdmp", k.c["qdmp"])

    def emit_finish(self):
        k = self
        k.barrier()

        @k.block.sync
        def _(q):
            k.bwait(q, "sync")
            q.nop()



def _prep_inputs(L0_attn_a_hidden_states, L0_attn_a_in_proj_w, L0_attn_a_conv_w,
                 L0_attn_a_conv_b, L0_attn_a_dt_bias, L0_attn_a_A_log, L0_attn_a_D,
                 L0_attn_a_norm_weight, L0_attn_a_out_proj_w, L0_attn_b_w_qkv,
                 L0_attn_b_b_qkv, L0_attn_b_w_out, L0_attn_b_b_out, L0_ffn_up_weight,
                 L0_ffn_down_weight, L0_ffn_up_scale, L0_ffn_down_scale,
                 L0_ffn_output_scale):
    f = np.float32
    winp = np.zeros((PROJ_P, D), f)
    winp[:8384] = np.asarray(L0_attn_a_in_proj_w, f)
    shared = {
        "winp": winp,
        "cwt": np.ascontiguousarray(
            np.asarray(L0_attn_a_conv_w, f).reshape(NCONV, 128, 4)
            .transpose(1, 0, 2).reshape(128, NCONV * 4)),
        "cbt": np.ascontiguousarray(np.asarray(L0_attn_a_conv_b, f).reshape(NCONV, 128).T),
        "dtb": np.asarray(L0_attn_a_dt_bias, f).reshape(64, 1),
        "alog": np.asarray(L0_attn_a_A_log, f).reshape(64, 1),
        "dvec": np.asarray(L0_attn_a_D, f).reshape(64, 1),
        "woutm": np.asarray(L0_attn_a_out_proj_w, f),
        "wqkv": np.asarray(L0_attn_b_w_qkv, f),
        "bqkvt": np.ascontiguousarray(np.asarray(L0_attn_b_b_qkv, f).reshape(48, 128).T),
        "wouta": np.asarray(L0_attn_b_w_out, f),
        "boutbc": np.ascontiguousarray(
            np.broadcast_to(np.asarray(L0_attn_b_b_out, f)[None, :], (128, D))),
        "wup": np.asarray(L0_ffn_up_weight, f),
        "wdn": np.asarray(L0_ffn_down_weight, f),
        "upsct": np.ascontiguousarray(np.asarray(L0_ffn_up_scale, f).reshape(64, 128).T),
        "dsct": np.ascontiguousarray(np.asarray(L0_ffn_down_scale, f).reshape(16, 128).T),
        "osct": np.ascontiguousarray(
            np.broadcast_to(np.asarray(L0_ffn_output_scale, f).reshape(1, 1), (128, 16))),
        "nwbc": np.ascontiguousarray(
            np.broadcast_to(np.asarray(L0_attn_a_norm_weight, f)[None, :], (128, INTER))),
        "idbf": np.eye(128).astype(ml_dtypes.bfloat16),
        "idf": np.eye(128, dtype=f),
        "ones1": np.ones((1, 128), ml_dtypes.bfloat16),
        "onec": np.ones((128, 1), ml_dtypes.bfloat16),
        "smlog": np.where(np.arange(128)[:, None] <= np.arange(128)[None, :],
                          0.0, MASK_NEG).astype(f),
    }
    x = np.asarray(L0_attn_a_hidden_states, f)
    return [dict(shared, x0=np.ascontiguousarray(x[b])) for b in range(x.shape[0])]


def kernel(**inputs):
    in_maps = _prep_inputs(**inputs)
    kb = KB(L=4096, phases=ALL_PHASES)
    nc = kb.build()
    res = run_bass_kernel_spmd(nc, in_maps, core_ids=list(range(len(in_maps))))
    return np.stack([r["y_out"] for r in res.results], axis=0)
